# version 14
# baseline (speedup 1.0000x reference)
import numpy as np
from contextlib import ExitStack
import concourse.bass as bass
import concourse.mybir as mybir
from concourse.bass_utils import run_bass_kernel_spmd
from concourse.bass_core_rust import IndirectOffsetOnAxis

F32 = mybir.dt.float32
BF16 = mybir.dt.bfloat16
I32 = mybir.dt.int32
AF = mybir.ActivationFunctionType
AL = mybir.AluOpType

ENGINES = ("tensor", "vector", "scalar", "sync", "gpsimd")

NPOOL = 6

class KB:
    def __init__(self, nc, es):
        self.nc = nc
        self.es = es
        self.streams = {e: [] for e in ENGINES}
        self.sems = {}
        self.cnt = {}
        for s in ("te", "ve", "se", "gc", "cc"):
            self.sems[s] = es.enter_context(nc.semaphore(s))
            self.cnt[s] = 0
        self.pools = {}
        self.qctr = {}
        self.qhist = {}
        for q in ("sy", "gp"):
            self.pools[q] = []
            for i in range(NPOOL):
                s = f"{q}{i}"
                self.sems[s] = es.enter_context(nc.semaphore(s))
                self.cnt[s] = 0
                self.pools[q].append(s)
            self.qctr[q] = 0
            self.qhist[q] = []

    def dma(self, eng, fn, queue, waits=()):
        i = self.qctr[queue]
        s = self.pools[queue][i % NPOOL]
        w = [x for x in waits if x is not None]
        if i >= NPOOL:
            w.append(self.qhist[queue][i - NPOOL])
        self.cnt[s] += 16
        ev = (s, self.cnt[s])
        self.streams[eng].append((fn, w, [(s, 16)]))
        self.qhist[queue].append(ev)
        self.qctr[queue] = i + 1
        return ev

    def qall(self, queue):
        h = self.qhist[queue]
        return h[-NPOOL:] if h else []

    def emit(self, eng, fn, waits=(), inc=None, n=1, dr=False):
        incs = []
        if inc is not None:
            self.cnt[inc] += n
            incs = [(inc, n)]
        if dr:
            fn0 = fn
            fn = (lambda e, fn0=fn0: (e.drain(), fn0(e))[1])
        self.streams[eng].append((fn, [w for w in waits if w is not None], incs))
        return (inc, self.cnt[inc]) if inc is not None else None

    def ev(self, s):
        return (s, self.cnt[s])

    def finalize(self):
        block = self.es.enter_context(self.nc.Block())
        def mk(engname):
            def body(eng):
                lw = {}
                for fn, waits, incs in self.streams[engname]:
                    for (s, v) in waits:
                        if v > lw.get(s, 0):
                            eng.wait_ge(self.sems[s], v)
                            lw[s] = v
                    ins = fn(eng)
                    for (s, n) in incs:
                        assert ins is not None, f"{engname} inc without instruction"
                        ins.then_inc(self.sems[s], n)
            return body
        block.sync(mk("sync"))
        block.gpsimd(mk("gpsimd"))
        block.tensor(mk("tensor"))
        block.vector(mk("vector"))
        block.scalar(mk("scalar"))


HB, HT, HD_, HH, HDH, HNE, HHD2 = 4, 4096, 1024, 16, 64, 16, 128
HNC = 8
OWN = 2048
NT = 16
NG = 4
EPS = 1e-5

def own_chunks(c): return (0, 3) if c % 2 == 0 else (1, 2)
def own_qblks(c):  return (0, 1, 6, 7) if c % 2 == 0 else (2, 3, 4, 5)
D_SLOTS = (12, 16, 28, 32)
def d_slot_qblks(c):
    qb = sorted(own_qblks(c), key=lambda q: q)
    return qb

def host_prep(inputs):
    x = np.asarray(inputs["L0_attn_x"], dtype=np.float32)
    Wqkv = np.asarray(inputs["L0_attn_W_qkv"], dtype=np.float32)
    Wo = np.asarray(inputs["L0_attn_W_o"], dtype=np.float32)
    gamma = np.asarray(inputs["L0_attn_ln_gamma"], dtype=np.float32)
    beta = np.asarray(inputs["L0_attn_ln_beta"], dtype=np.float32)
    sim = np.asarray(inputs["L0_ffn_sim_matrix"], dtype=np.float32)
    gates = np.asarray(inputs["L0_ffn_gates"], dtype=np.float32)
    qp = np.asarray(inputs["L0_ffn_q_proj"], dtype=np.float32)
    kp = np.asarray(inputs["L0_ffn_k_proj"], dtype=np.float32)
    vp = np.asarray(inputs["L0_ffn_v_proj"], dtype=np.float32)
    op = np.asarray(inputs["L0_ffn_o_proj"], dtype=np.float32)

    perm = []
    for t in range(3):
        for hp in range(8):
            for hh in range(2):
                h = 2 * hp + hh
                base = h * 192 + t * 64
                perm.extend(range(base, base + 64))
    WqkvT = np.ascontiguousarray(Wqkv[perm].T)
    WoT = np.ascontiguousarray(Wo.T)
    g8 = np.ascontiguousarray(gamma.reshape(8, 128).T)
    b8 = np.ascontiguousarray(beta.reshape(8, 128).T)

    qpr = np.ascontiguousarray(qp.transpose(1, 0, 2).reshape(1024, 16 * 128))
    kpr = np.ascontiguousarray(kp.transpose(1, 0, 2).reshape(1024, 16 * 128))
    vpr = np.ascontiguousarray(vp.transpose(1, 0, 2).reshape(1024, 16 * 128))
    opr = np.ascontiguousarray(op.transpose(1, 0, 2).reshape(128, 16 * 1024))
    per_core = []
    for c in range(HNC):
        b = c // 2
        ch = own_chunks(c)
        rows = np.concatenate([np.arange(ch[0]*1024, ch[0]*1024+1024),
                               np.arange(ch[1]*1024, ch[1]*1024+1024)])
        x_own = np.ascontiguousarray(x[b][rows])

        def chunk_rank_j(g):
            r_even, r_odd = 2*b, 2*b+1
            if g in (0, 3): return (r_even, 0 if g == 0 else 1)
            else:           return (r_odd, 0 if g == 1 else 1)
        hp_loc = [c % 2 * 4 + i for i in range(4)]
        idx_qkv = np.zeros((128, 80), dtype=np.int32)
        for i in range(4):
            hpg = hp_loc[i]
            for t in range(2):
                for h in range(2):
                    for g in range(4):
                        r, j = chunk_rank_j(g)
                        d0 = t * 1024 + hpg * 128 + h * 64
                        cc = ((i * 2 + t) * 2 + h) * 4 + g
                        idx_qkv[0:64, cc] = r * 3072 + d0 + np.arange(64)
            for g in range(4):
                r, j = chunk_rank_j(g)
                d0 = 2 * 1024 + hpg * 128
                idx_qkv[:, 64 + i * 4 + g] = r * 3072 + d0 + np.arange(128)
        idx_ctx = np.zeros((128, 16), dtype=np.int32)
        for dc in range(8):
            dims = dc * 128 + np.arange(128)
            rr = 2*b + (dims >= 512)
            for s in range(2):
                g = ch[s]
                idx_ctx[:, dc * 2 + s] = (rr * 128 + np.arange(128)) * 4 + g
        idx_kv2 = np.zeros((128, 8), dtype=np.int32)
        for t in range(2):
            for g in range(4):
                r, j = chunk_rank_j(g)
                idx_kv2[:, t * 4 + g] = (r * 256 + t * 128 + np.arange(128)) * 2 + j
        mlen = np.zeros((128, 128), dtype=np.float32)
        qbs = d_slot_qblks(c)
        for sp in range(4):
            gq = qbs[sp]
            for kt in range(32):
                mlen[:, sp * 32 + kt] = gq * 512 - kt * 128 - np.arange(128) + 1
        per_core.append(dict(
            x_own=x_own, idx_qkv=idx_qkv, idx_ctx=idx_ctx, idx_kv2=idx_kv2, mlen=mlen,
            WqkvT=WqkvT, WoT=WoT, g8=g8, b8=b8, sim=sim, gates=gates.reshape(1, HNE),
            qp=qpr, kp=kpr, vp=vpr, op=opr,
        ))
    return per_core

def assemble_out(results):
    out = np.zeros((HB, HT, HD_), dtype=np.float32)
    for c in range(HNC):
        b = c // 2
        ch = own_chunks(c)
        r = results[c]["out_own"]
        out[b, ch[0]*1024:(ch[0]+1)*1024] = r[0:1024]
        out[b, ch[1]*1024:(ch[1]+1)*1024] = r[1024:2048]
    return out


Bb, T, D, H, DH, NE, HD2 = 4, 4096, 1024, 16, 64, 16, 128
OWN, NT, NG = 2048, 16, 4
EPS = 1e-5
D_SLOTS = (12, 16, 28, 32)

def build(nc, dbg=(), upto='full'):
    es = ExitStack(); es.__enter__()
    kb = KB(nc, es)
    S = lambda name, shape, dt=F32: es.enter_context(nc.sbuf_tensor(name, shape, dt))
    PS = lambda name, shape, dt=F32: es.enter_context(nc.psum_tensor(name, shape, dt))

    dr_in = lambda n, sh, dt=F32: nc.dram_tensor(n, sh, dt, kind="ExternalInput").ap()
    x_own = dr_in("x_own", [OWN, D])
    WqkvT = dr_in("WqkvT", [D, 3 * D])
    WoT = dr_in("WoT", [D, D])
    g8 = dr_in("g8", [128, 8]); b8 = dr_in("b8", [128, 8])
    sim = dr_in("sim", [D, NE]); gates = dr_in("gates", [1, NE])
    qp = dr_in("qp", [D, NE * HD2]); kp = dr_in("kp", [D, NE * HD2]); vp = dr_in("vp", [D, NE * HD2])
    opw = dr_in("opw", [HD2, NE * D])
    idx_qkv = dr_in("idx_qkv", [128, 80], I32)
    idx_ctx = dr_in("idx_ctx", [128, 16], I32)
    idx_kv2 = dr_in("idx_kv2", [128, 8], I32)
    mlen = dr_in("mlen", [128, 128])
    out_own = nc.dram_tensor("out_own", [OWN, D], F32, kind="ExternalOutput").ap()

    pay1q = [nc.dram_tensor(f"pay1q{k}", [3 * D, 512], BF16, kind="Internal").ap() for k in range(4)]
    ag1q = [nc.dram_tensor(f"ag1q{k}", [8 * 3 * D, 512], BF16, kind="Internal", addr_space="Shared").ap() for k in range(4)]
    payc = nc.dram_tensor("payc", [512, T], BF16, kind="Internal").ap()
    agcs = [nc.dram_tensor(f"agc{i}", [8 * 128, T], BF16, kind="Internal", addr_space="Shared").ap() for i in range(4)]
    pay2 = nc.dram_tensor("pay2", [2 * HD2, OWN], BF16, kind="Internal").ap()
    ag2 = nc.dram_tensor("ag2", [8 * 2 * HD2, OWN], BF16, kind="Internal", addr_space="Shared").ap()
    x2scr = nc.dram_tensor("x2scr", [OWN, D], F32, kind="Internal").ap()
    agcvs = [a.rearrange("r (j t) -> (r j) t", j=4) for a in agcs]
    ag2v = ag2.rearrange("r (j t) -> (r j) t", j=2)

    dbg_t = {}
    dbg_specs = {"xnT": ([128, 8 * OWN], BF16), "pay1": ([3 * D, OWN], BF16),
                 "qt": ([3 * 128, T], BF16), "eb": ([128, 4096], BF16),
                 "ctx": ([512, T], BF16), "x2": ([OWN, D], F32), "h": ([OWN, D], F32),
                 "rw": ([OWN, NE], F32), "qkv2": ([OWN, 3 * HD2], F32),
                 "ao": ([HD2, OWN], BF16), "cb": ([1, 16], F32)}
    for k in dbg:
        sh, dt = dbg_specs[k]
        dbg_t[k] = nc.dram_tensor("d_" + k, sh, dt, kind="ExternalOutput").ap()

    cpool = S("cpool", [128, 32], F32)
    cvals = {}
    def constf(v):
        v = float(v)
        if v not in cvals:
            i = len(cvals)
            assert i < 32
            cvals[v] = i
            nc.const_aps.register(F32, v, cpool[:, i:i+1])
        return v

    eyeb = S("eyeb", [128, 128], BF16)
    eyef = S("eyef", [128, 128], F32)
    onesb = S("onesb", [128, 128], BF16)
    ones1f = S("ones1f", [1, 128], F32)
    ones1b = S("ones1b", [1, 128], BF16)
    coliota = S("coliota", [128, 512], F32)
    rowv = S("rowv", [128, 1], F32)
    maskB = S("maskB", [128, 4 * 1024], BF16)
    offsD = S("offsD", [128, 512], F32)
    oh16 = S("oh16", [16, 16 * 128], BF16)
    rv16 = S("rv16", [16, 1], F32)

    kb.emit("gpsimd", lambda e: e.iota(coliota[:], [[1, 512]], base=0, channel_multiplier=0, allow_small_or_imprecise_dtypes=True))
    kb.emit("gpsimd", lambda e: e.iota(rowv[:], [[0, 1]], base=0, channel_multiplier=1, allow_small_or_imprecise_dtypes=True))
    kb.emit("gpsimd", lambda e: e.iota(offsD[:], [[-1, 512]], base=0, channel_multiplier=0, allow_small_or_imprecise_dtypes=True))
    kb.emit("gpsimd", lambda e: e.iota(oh16[:], [[1, 16], [0, 128]], base=0, channel_multiplier=0, allow_small_or_imprecise_dtypes=True))
    kb.emit("gpsimd", lambda e: e.iota(rv16[:], [[0, 1]], base=0, channel_multiplier=1, allow_small_or_imprecise_dtypes=True))
    kb.emit("gpsimd", lambda e: e.memset(onesb[:], 1.0))
    kb.emit("gpsimd", lambda e: e.memset(ones1f[:], 1.0))
    kb.emit("gpsimd", lambda e: e.memset(ones1b[:], 1.0), inc="gc")
    gcev = kb.ev("gc")
    constf(EPS)
    constf(1e-12)
    def emit_cpool():
        pass
    kb.emit("vector", lambda e: e.tensor_scalar(eyef[:], coliota[:, 0:128], rowv[:], None, AL.is_equal), waits=[gcev])
    kb.emit("vector", lambda e: e.tensor_scalar(oh16[:], oh16[:], rv16[:], None, AL.is_equal), dr=True)
    kb.emit("vector", lambda e: e.tensor_copy(eyeb[:], eyef[:]), dr=True)
    for j in range(4):
        kb.emit("vector", lambda e, j=j: e.tensor_scalar(maskB[:, j*1024:j*1024+512], coliota[:], rowv[:], float(128*j), AL.subtract, AL.is_ge))
        kb.emit("vector", lambda e, j=j: e.tensor_scalar(maskB[:, j*1024+512:(j+1)*1024], coliota[:], rowv[:], float(128*j), AL.subtract, AL.is_ge))
    consts_ev = kb.emit("vector", lambda e: e.tensor_copy(eyeb[:], eyef[:]), inc="ve", dr=True)

    kb.emit("scalar", lambda e: e.nop(), waits=[gcev])
    kb.emit("vector", lambda e: e.nop(), waits=[gcev])
    xa = [S(f"xa{i}", [128, D]) for i in range(2)]
    ya = S("ya", [128, D])
    sqscr = S("sqscr", [128, D], BF16)
    xnp = S("xnp", [128, D], BF16)
    smalls = S("smalls", [128, 64])
    esA = ExitStack(); esA.__enter__()
    SA = lambda name, shape, dt=F32: esA.enter_context(nc.sbuf_tensor(name, shape, dt))
    PSA = lambda name, shape, dt=F32: esA.enter_context(nc.psum_tensor(name, shape, dt))
    Wqkvb = SA("Wqkvb", [128, 8 * 3 * D], BF16)
    for dc in range(8):
        kb.dma("gpsimd", lambda e, dc=dc: e.dma_start(out=Wqkvb[:, dc*3072:(dc+1)*3072], in_=WqkvT[dc*128:(dc+1)*128, :]), "gp")
    wqkv_ev = kb.qall("gp")
    g8b = SA("g8b", [128, 8]); b8b = SA("b8b", [128, 8])
    gb_ev0 = kb.dma("sync", lambda e: e.dma_start(out=g8b[:], in_=g8), "sy")
    gb_ev = kb.dma("sync", lambda e: e.dma_start(out=b8b[:], in_=b8), "sy")

    xnT = SA("xnT", [128, 8 * 512], BF16)
    psT = [PSA(f"psT{i}", [128, 512], BF16) for i in range(2)]
    qkps = [PSA(f"qkps{i}", [128, 512]) for i in range(2)]
    qstage = [SA(f"qstage{i}", [128, 512], BF16) for i in range(2)]
    col = lambda tns, i: tns[:, i:i+1]

    def ln_stats(src, base, waitv):
        bns = smalls[:, base:base+12]
        mu = col(smalls, base+12)
        var = col(smalls, base+13)
        sd = col(smalls, base+14)
        r = col(smalls, base+15)
        kb.emit("vector", lambda e: e.bn_stats(bns[:, 0:6], src[:, 0:512]), waits=waitv, dr=True)
        kb.emit("vector", lambda e: e.bn_stats(bns[:, 6:12], src[:, 512:1024]), dr=True)
        vev = kb.emit("vector", lambda e: e.bn_aggr(smalls[:, base+12:base+14], bns[:]), inc="ve", dr=True)
        sdev = kb.emit("scalar", lambda e: e.activation(sd, var, AF.Sqrt, bias=EPS), waits=[vev], inc="se")
        vev2 = kb.emit("vector", lambda e: e.reciprocal(r, sd), waits=[sdev], inc="ve")
        return vev2, mu, r

    pay_evs = []
    ccq = []
    dump_gate = []
    prev_evac = None
    yev_hist = []
    for t in range(NT):
        g, tt = t // 4, t % 4
        pb = (t % 2) * 32
        buf = xa[t % 2]
        ldw = [consts_ev]
        if t >= 2: ldw.append(yev_hist[t - 2])
        ldev = kb.dma("sync", lambda e, buf=buf, t=t: e.dma_start(out=buf[:], in_=x_own[t*128:(t+1)*128, :]), "sy", waits=ldw)
        ev1, mu1, r1 = ln_stats(buf, pb, [ldev])
        yev = kb.emit("vector", lambda e, buf=buf, mu1=mu1, r1=r1: e.tensor_scalar(ya[:], buf[:], mu1, r1, AL.subtract, AL.mult), waits=[ev1], inc="ve", dr=True)
        yev_hist.append(yev)
        ev2, mu2, r2 = ln_stats(ya, pb + 16, [yev])
        xev = kb.emit("vector", lambda e, mu2=mu2, r2=r2: e.tensor_scalar(xnp[:], ya[:], mu2, r2, AL.subtract, AL.mult), waits=[ev2], inc="ve", dr=True)
        tev = None
        for dc in range(8):
            bank, jj = psT[dc // 4], (dc % 4) * 128
            w = ([xev] + ([prev_evac] if prev_evac else [])) if dc == 0 else ()
            tev = kb.emit("tensor", lambda e, dc=dc, bank=bank, jj=jj: e.transpose(bank[:, jj:jj+128], xnp[:, dc*128:(dc+1)*128], eyeb[:]), waits=w, inc="te")
        vev = None
        for dc in range(8):
            bank, jj = psT[dc // 4], (dc % 4) * 128
            w = ([tev, gb_ev, gb_ev0] + dump_gate) if dc == 0 else ()
            vev = kb.emit("vector", lambda e, dc=dc, bank=bank, jj=jj, tt=tt: e.tensor_scalar(
                xnT[:, dc*512 + tt*128: dc*512 + tt*128 + 128], bank[:, jj:jj+128],
                g8b[:, dc:dc+1], b8b[:, dc:dc+1], AL.mult, AL.add), waits=w, inc="ve")
        prev_evac = vev
        if "xnT" in dbg_t:
            for dc in range(8):
                kb.dma("gpsimd", lambda e, dc=dc, t=t, tt=tt: e.dma_start(
                    out=dbg_t["xnT"][:, dc*2048 + t*128: dc*2048 + t*128 + 128],
                    in_=xnT[:, dc*512 + tt*128: dc*512 + tt*128 + 128]), "gp", waits=[vev])
            if tt == 3:
                dump_gate = kb.qall("gp")
        if tt == 3:
            grp_ev = vev
            for oc in range(24):
                ps, stg = qkps[oc % 2], qstage[oc % 2]
                mmev = None
                for dc in range(8):
                    w = ()
                    if dc == 0:
                        w = [grp_ev] + list(wqkv_ev) + ([pay_evs[-2][1]] if len(pay_evs) >= 2 else [])
                    mmev = kb.emit("tensor", lambda e, oc=oc, dc=dc, ps=ps: e.matmul(
                        ps[:], Wqkvb[:, dc*3072 + oc*128: dc*3072 + oc*128 + 128],
                        xnT[:, dc*512:(dc+1)*512], start=(dc == 0), stop=(dc == 7)), waits=w, inc="te")
                scev = kb.emit("scalar", lambda e, ps=ps, stg=stg: e.copy(stg[:], ps[:]),
                               waits=[mmev] + ([pay_evs[-2][0]] if len(pay_evs) >= 2 else []), inc="se")
                dmev = kb.dma("sync", lambda e, oc=oc, stg=stg, pdst=pay1q[g]: e.dma_start(
                    out=pdst[oc*128:(oc+1)*128, :], in_=stg[:]), "sy", waits=[scev])
                pay_evs.append((dmev, scev))

            ccq.append(kb.emit("gpsimd", lambda e, g=g: e.collective_compute(
                "AllGather", AL.bypass, replica_groups=[list(range(8))], ins=[pay1q[g]], outs=[ag1q[g]]),
                waits=kb.qall("sy"), inc="cc"))
            prev_evac = kb.ev("te")

    ccev = ccq[-1]

    esA.__exit__(None, None, None)
    a_done_te = kb.ev("te")
    esB = ExitStack(); esB.__enter__()
    S_ = lambda name, shape, dt=F32: esB.enter_context(nc.sbuf_tensor(name, shape, dt))
    PS_ = lambda name, shape, dt=F32: esB.enter_context(nc.psum_tensor(name, shape, dt))
    QTa = [S_(f"QTa{i}", [65, 2 * T], BF16) for i in range(2)]
    K2Ta = [S_(f"K2Ta{i}", [65, 2 * T], BF16) for i in range(2)]
    VTr = [S_(f"VTr{i}", [128, T], BF16) for i in range(2)]
    Vt = [S_(f"Vt{i}", [128, T], BF16) for i in range(2)]
    Ebuf = S_("Ebuf", [128, 4 * 1024], BF16)
    nsq = S_("nsq", [64, 2 * T], BF16)
    knm = S_("knm", [128, 8])
    ctxst = [S_(f"ctxst{h}", [64, 512], BF16) for h in range(2)]
    alb2 = S_("alb2", [64, 512])
    psSS = PS_("psSS", [128, 2048])
    psC = [PS_(f"psC{h}", [128, 512]) for h in range(2)]
    psL2 = [PS_(f"psL{h}", [128, 512]) for h in range(2)]

    idxs = S_("idxs", [128, 80], I32)
    idxs_ev = kb.dma("sync", lambda e: e.dma_start(out=idxs[:], in_=idx_qkv), "sy", waits=[pay_evs[-1][0], a_done_te])
    kb.emit("gpsimd", lambda e: e.memset(K2Ta[0][64:65, :], 1.0), inc="gc")
    kb.emit("gpsimd", lambda e: e.memset(K2Ta[1][64:65, :], 1.0), inc="gc")
    ones_ev = kb.ev("gc")

    J4 = (0, 1, 0, 1, 2, 3, 2, 3)
    def load_hp(i, buf, waits):
        firstq = [True] * 4
        def gw(j4):
            if firstq[j4]:
                firstq[j4] = False
                return [idxs_ev, ccq[j4]] + list(waits)
            return []
        for t3 in range(2):
            dst = (QTa, K2Ta)[t3][buf]
            for h in range(2):
                for g in range(4):
                    cc = ((i * 2 + t3) * 2 + h) * 4 + g
                    for hf in range(2):
                        b8 = g * 2 + hf
                        j4 = J4[b8]
                        kb.dma("gpsimd", lambda e, cc=cc, dst=dst, h=h, g=g, hf=hf, j4=j4: e.indirect_dma_start(
                            out=dst[0:64, h*T + g*1024 + hf*512: h*T + g*1024 + (hf+1)*512], out_offset=None,
                            in_=ag1q[j4], in_offset=IndirectOffsetOnAxis(idxs[0:64, cc:cc+1], 0)), "gp", waits=gw(j4))
        for g in range(4):
            cc = 64 + i * 4 + g
            for hf in range(2):
                b8 = g * 2 + hf
                j4 = J4[b8]
                kb.dma("gpsimd", lambda e, cc=cc, buf=buf, g=g, hf=hf, j4=j4: e.indirect_dma_start(
                    out=VTr[buf][:, g*1024 + hf*512: g*1024 + (hf+1)*512], out_offset=None,
                    in_=ag1q[j4], in_offset=IndirectOffsetOnAxis(idxs[:, cc:cc+1], 0)), "gp", waits=gw(j4))
        return kb.qall("gp")

    cc2s = []
    ldB = [None, None]
    ldB[0] = load_hp(0, 0, [])
    hp_done = []
    ctxdma_evs = []
    hp_prev_sclear = None
    for i in range(4):
        buf = i % 2
        ld_ev = ldB[buf]
        if i + 1 < 4:
            w = [hp_done[i - 1]] if i >= 1 else []
            ldB[(i + 1) % 2] = load_hp(i + 1, (i + 1) % 2, w)
        vt_last = None
        for kt in range(32):
            w = list(ld_ev) if kt == 0 else ()
            if i >= 2 and kt == 0:
                w = list(ld_ev) + [hp_done[i - 2]]
            vt_last = kb.dma("sync", lambda e, kt=kt, buf=buf: e.dma_start(
                out=Vt[buf][:, kt*128:(kt+1)*128], in_=VTr[buf][:, kt*128:(kt+1)*128], transpose=True),
                "sy", waits=w)
        vt_all_b = kb.qall("sy")
        sqev = kb.emit("scalar", lambda e, buf=buf: e.activation(nsq[:], K2Ta[buf][0:64, :], AF.Square),
                       waits=list(ld_ev) + ([hp_prev_sclear] if hp_prev_sclear else []), inc="se")
        rdev = None
        for rnd in range(4):
            mme = None
            for k2 in range(4):
                cb = rnd * 4 + k2
                w = ()
                if k2 == 0:
                    w = [sqev, vt_last] if rnd == 0 else [rdev]
                mme = kb.emit("tensor", lambda e, cb=cb, k2=k2: e.matmul(
                    psSS[:, k2*512:(k2+1)*512], onesb[0:64, :], nsq[0:64, cb*512:(cb+1)*512],
                    start=True, stop=True), waits=w, inc="te")
            rdev = kb.emit("vector", lambda e, rnd=rnd: e.reduce_max(
                knm[:, rnd:rnd+1], psSS[:], axis=mybir.AxisListType.X), waits=[mme], inc="ve", dr=True)
        k2ev = kb.emit("vector", lambda e: e.reduce_max(knm[:, 4:5], knm[:, 0:4], axis=mybir.AxisListType.X), dr=True, inc="ve")
        ksev = kb.emit("scalar", lambda e: e.activation(knm[:, 5:6], knm[:, 4:5], AF.Sqrt), waits=[k2ev], inc="se")
        kmev = kb.emit("vector", lambda e: e.tensor_scalar(knm[:, 6:7], knm[:, 5:6], -1.0, None, AL.mult),
                       waits=[ksev], inc="ve", dr=True)
        if "cb" in dbg_t and i == 0:
            kb.dma("gpsimd", lambda e: e.dma_start(out=dbg_t["cb"][0:1, 0:8], in_=knm[0:1, 0:8]), "gp", waits=[kmev])
        brow_evs = []
        sq2ev = None
        for h in range(2):
            sq2ev = kb.emit("scalar", lambda e, buf=buf, h=h: e.activation(
                nsq[0:64, 0:T], QTa[buf][0:64, h*T:(h+1)*T], AF.Square),
                waits=list(ld_ev) + ([rdev] if h == 0 else list(brow_evs)), inc="se")
            sqrt_evs = []
            for cb in range(8):
                bank = cb % 4
                w = [sq2ev, kmev] if cb == 0 else ([sqrt_evs[cb - 4]] if cb >= 4 else ())
                mme = kb.emit("tensor", lambda e, cb=cb, bank=bank: e.matmul(
                    psSS[:, bank*512:(bank+1)*512], onesb[0:64, :], nsq[0:64, cb*512:(cb+1)*512],
                    start=True, stop=True), waits=w, inc="te")
                sev_ = kb.emit("scalar", lambda e, cb=cb, bank=bank, h=h, buf=buf: e.activation(
                    QTa[buf][64:65, h*T + cb*512: h*T + (cb+1)*512], psSS[64:65, bank*512:(bank+1)*512], AF.Sqrt),
                    waits=[mme], inc="se")
                sqrt_evs.append(sev_)
            bev_ = kb.emit("vector", lambda e, h=h, buf=buf: e.tensor_scalar(
                QTa[buf][64:65, h*T:(h+1)*T], QTa[buf][64:65, h*T:(h+1)*T],
                knm[64:65, 6:7], 480.0, AL.mult, AL.add), waits=[sqrt_evs[-1]], inc="ve", dr=True)
            brow_evs.append(bev_)
        exp_evs = {}; cons_evs = {}
        mask_evs = {}
        ctx_norm_prev = None
        store_prev = [None, None]
        for q in range(8):
            nkt = 4 * q + 4
            nu = 2 * nkt
            avev = None
            def emit_av(u, q=q, nkt=nkt, buf=buf):
                kt, h = u // 2, u % 2
                esl = (u % 8) * 512
                st = (kt == 0); sp = (kt == nkt - 1)
                kb.emit("tensor", lambda e, h=h, kt=kt, esl=esl, st=st, sp=sp, buf=buf: e.matmul(
                    psC[h][0:64, :], Vt[buf][:, kt*128 + h*64: kt*128 + (h+1)*64],
                    Ebuf[:, esl:esl+512], start=st, stop=sp),
                    waits=[mask_evs[u]], inc="te")
                av = kb.emit("tensor", lambda e, h=h, esl=esl, st=st, sp=sp: e.matmul(
                    psL2[h][0:64, :], onesb[:, 0:64], Ebuf[:, esl:esl+512], start=st, stop=sp),
                    inc="te")
                cons_evs[u] = av
                return av
            for u in range(nu):
                kt, h = u // 2, u % 2
                scol = (u % 4) * 512
                esl = (u % 8) * 512
                w = []
                if u >= 4: w.append(exp_evs[u - 4])
                if u == 0:
                    w += [brow_evs[-1], ones_ev] + list(vt_all_b) + list(ld_ev)
                    if ctx_norm_prev is not None: w.append(ctx_norm_prev)
                mm2 = kb.emit("tensor", lambda e, kt=kt, q=q, h=h, scol=scol, buf=buf: e.matmul(
                    psSS[:, scol:scol+512],
                    K2Ta[buf][0:65, h*T + kt*128: h*T + (kt+1)*128],
                    QTa[buf][0:65, h*T + q*512: h*T + (q+1)*512], start=True, stop=True),
                    waits=w, inc="te")
                ew = [mm2]
                if u >= 8: ew.append(cons_evs[u - 8])
                eev = kb.emit("scalar", lambda e, scol=scol, esl=esl: e.activation(
                    Ebuf[:, esl:esl+512], psSS[:, scol:scol+512], AF.Exp, scale=0.125),
                    waits=ew, inc="se")
                exp_evs[u] = eev
                mev = eev
                if kt >= 4 * q:
                    j = kt - 4 * q
                    mev = kb.emit("vector", lambda e, j=j, h=h, esl=esl: e.tensor_tensor(
                        Ebuf[:, esl:esl+512], Ebuf[:, esl:esl+512], maskB[:, j*1024 + h*512: j*1024 + (h+1)*512], AL.mult),
                        waits=[eev], inc="ve", dr=True)
                mask_evs[u] = mev
                if u >= 4:
                    avev = emit_av(u - 4)
            for u in range(max(0, nu - 4), nu):
                avev = emit_av(u)
            nev = None
            for h in range(2):
                rev = kb.emit("vector", lambda e, h=h: e.tensor_scalar(
                    alb2[:], psL2[h][0:64, :], 1.2e-38, None, AL.max),
                    waits=[avev], inc="ve", dr=True)
                rev = kb.emit("vector", lambda e, h=h: e.reciprocal(alb2[:], alb2[:]), waits=[rev], inc="ve", dr=True)
                nev = kb.emit("vector", lambda e, h=h: e.tensor_tensor(ctxst[h][:], psC[h][0:64, :], alb2[:], AL.mult),
                              waits=[rev] + ([store_prev[h]] if store_prev[h] else []), inc="ve", dr=True)
                dev_ = kb.dma("sync", lambda e, i=i, q=q, h=h: e.dma_start(
                    out=payc[i*128 + h*64: i*128 + (h+1)*64, q*512:(q+1)*512], in_=ctxst[h][:]),
                    "sy", waits=[nev])
                store_prev[h] = dev_
                ctxdma_evs.append(dev_)
            ctx_norm_prev = nev
        hp_prev_sclear = exp_evs[max(exp_evs)]
        hp_done.append(kb.ev("ve"))
        cc2s.append(kb.emit("gpsimd", lambda e, i=i: e.collective_compute(
            "AllGather", AL.bypass, replica_groups=[list(range(8))],
            ins=[payc[i*128:(i+1)*128, :]], outs=[agcs[i]]), waits=kb.qall("sy"), inc="cc"))
    if "ctx" in dbg_t:
        for i in range(4):
            kb.dma("gpsimd", lambda e, i=i: e.dma_start(out=dbg_t["ctx"][i*128:(i+1)*128, :], in_=payc[i*128:(i+1)*128, :]),
                    "gp", waits=kb.qall("sy"))

    cc2ev = cc2s[-1]
    b_done_te = kb.ev("te")
    if upto == "bb":
        kb.emit("sync", lambda e: e.nop(), waits=[cc2ev, kb.ev("se"), kb.ev("ve"), b_done_te] + kb.qall("gp") + kb.qall("sy"))
        kb.finalize()
        return es, dbg_t

    esB.__exit__(None, None, None)
    esC = ExitStack(); esC.__enter__()
    SC = lambda name, shape, dt=F32: esC.enter_context(nc.sbuf_tensor(name, shape, dt))
    PSC = lambda name, shape, dt=F32: esC.enter_context(nc.psum_tensor(name, shape, dt))
    ystg = [SC(f"ystg{k}", [128, 512], BF16) for k in range(2)]
    x2g = SC("x2g", [128, 4 * D])
    xrl = [SC(f"xrl{k}", [128, D]) for k in range(4)]
    hbuf = SC("hbuf", [128, D])
    hb16 = SC("hb16", [128, D], BF16)
    hlo16 = SC("hlo16", [128, D], BF16)
    simf = SC("simf", [128, 8 * 16])
    simhi = SC("simhi", [128, 8 * 16], BF16)
    simlo = SC("simlo", [128, 8 * 16], BF16)
    simsq = SC("simsq", [128, 16])
    gsm = SC("gsm", [128, 24])
    SIGB = SC("SIGB", [128, 16])
    NEG16 = SC("NEG16", [128, 16])
    invncol = SC("invncol", [128, 16])
    lgt = SC("lgt", [16, 512])
    rw = SC("rw", [128, 16 * 16])
    gtile = SC("gtile", [128, 80])
    onesf = SC("onesf", [128, 1])
    qtT = SC("qtT", [128, OWN], BF16)
    qkvb = SC("qkvb", [128, 128], BF16)
    aoT = SC("aoT", [128, OWN], BF16)
    rwb = SC("rwb", [128, 256], BF16)
    rwT = SC("rwT", [16, OWN], BF16)
    aoe = SC("aoe", [128, 16 * 512], BF16)
    kvst = aoe[:, 0:2 * OWN]
    psT2 = PSC("psT2", [128, 2048], BF16)
    psD = PSC("psD", [128, 2048])
    psDa = PSC("psDa", [128, 512])
    psDl = PSC("psDl", [128, 512])
    esOP = ExitStack(); esOP.__enter__()
    SOP = lambda name, shape, dt=F32: esOP.enter_context(nc.sbuf_tensor(name, shape, dt))
    WoTb = SOP("WoTb", [128, 8 * D], BF16)
    ctxo = SOP("ctxo", [128, 8 * OWN], BF16)
    idx2 = SOP("idx2", [128, 16], I32)
    psY = [psD[:, 0:512], psD[:, 512:1024]]

    for dc in range(8):
        kb.dma("gpsimd", lambda e, dc=dc: e.dma_start(out=WoTb[:, dc*D:(dc+1)*D], in_=WoT[dc*128:(dc+1)*128, :]),
               "gp", waits=[cc2ev] if dc == 0 else ())
    wo_ev = kb.qall("gp")
    idx2_ev = kb.dma("sync", lambda e: e.dma_start(out=idx2[:], in_=idx_ctx), "sy", waits=[cc2ev, b_done_te])
    gevs = []
    ccw = [False] * 4
    for dc in range(8):
        for s2 in range(2):
            hpb = dc % 4
            w = []
            if not ccw[hpb]:
                w = [idx2_ev, cc2s[hpb]]; ccw[hpb] = True
            gevs.append(kb.dma("gpsimd", lambda e, dc=dc, s2=s2, hpb=hpb: e.indirect_dma_start(
                out=ctxo[:, dc*OWN + s2*1024: dc*OWN + (s2+1)*1024], out_offset=None,
                in_=agcvs[hpb], in_offset=IndirectOffsetOnAxis(idx2[:, dc*2+s2:dc*2+s2+1], 0)),
                "gp", waits=w))
    ctxo_ev = kb.qall("gp")

    x2_store = []
    spill_evs = {}
    spill_tt = [None, None, None, None]
    for g in range(4):
        xev4 = []
        for tt in range(4):
            t = g * 4 + tt
            w = [spill_evs[g - 1]] if (g >= 1 and tt == 0) else []
            xev4.append(kb.dma("sync", lambda e, t=t, tt=tt: e.dma_start(out=xrl[tt][:], in_=x_own[t*128:(t+1)*128, :]),
                               "sy", waits=w))
        prev_tr = None
        addevs = []
        for oc in range(8):
            ps = psY[oc % 2]
            mmev = None
            for dc in range(8):
                w = ()
                if dc == 0:
                    w = (list(ctxo_ev) + list(wo_ev)) if (g == 0 and oc == 0) else [kb.ev("se")]
                mmev = kb.emit("tensor", lambda e, dc=dc, oc=oc, g=g, ps=ps: e.matmul(
                    ps, WoTb[:, dc*D + oc*128: dc*D + (oc+1)*128],
                    ctxo[:, dc*OWN + g*512:(dc*OWN + (g+1)*512)], start=(dc == 0), stop=(dc == 7)),
                    waits=w, inc="te")
            scev = kb.emit("scalar", lambda e, ps=ps, oc=oc: e.copy(ystg[oc % 2][:], ps), waits=[mmev], inc="se")
            trevs = []
            for tt in range(4):
                w = [scev]
                if prev_tr is not None and tt == 0:
                    w.append(prev_tr)
                trevs.append(kb.emit("tensor", lambda e, tt=tt, oc=oc: e.transpose(
                    psT2[:, (oc % 2)*1024 + tt*256: (oc % 2)*1024 + tt*256 + 128],
                    ystg[oc % 2][:, tt*128:(tt+1)*128], eyeb[:]), waits=w, inc="te"))
            lastadd = None
            for tt in range(4):
                w3 = [trevs[tt], xev4[tt]]
                if oc == 0 and spill_tt[tt] is not None:
                    w3.append(spill_tt[tt])
                lastadd = kb.emit("vector", lambda e, tt=tt, oc=oc, g=g: e.tensor_tensor(
                    x2g[:, tt*D + oc*128: tt*D + (oc+1)*128],
                    psT2[:, (oc % 2)*1024 + tt*256: (oc % 2)*1024 + tt*256 + 128],
                    xrl[tt][:, oc*128:(oc+1)*128], AL.add), waits=[trevs[3]] + w3, inc="ve")
            prev_tr = lastadd
            addevs.append(lastadd)
        sp_ev = None
        for tt in range(4):
            t = g * 4 + tt
            sp_ev = kb.dma("sync", lambda e, t=t, tt=tt: e.dma_start(
                out=x2scr[t*128:(t+1)*128, :], in_=x2g[:, tt*D:(tt+1)*D]), "sy", waits=[addevs[-1]])
            spill_tt[tt] = sp_ev
            if "x2" in dbg_t:
                kb.dma("gpsimd", lambda e, t=t, tt=tt: e.dma_start(
                    out=dbg_t["x2"][t*128:(t+1)*128, :], in_=x2g[:, tt*D:(tt+1)*D]), "gp", waits=[addevs[-1]])
        spill_evs[g] = sp_ev

    if upto == "op":
        kb.emit("sync", lambda e: e.nop(), waits=[kb.ev("se"), kb.ev("ve"), kb.ev("te")] + kb.qall("gp") + kb.qall("sy"))
        kb.finalize()
        return es, dbg_t
    op_done_te = kb.ev("te")
    esOP.__exit__(None, None, None)
    esC2 = ExitStack(); esC2.__enter__()
    SC2_ = lambda name, shape, dt=F32: esC2.enter_context(nc.sbuf_tensor(name, shape, dt))
    psL3 = psDa[0:16, :]
    psG = psDl
    hT = SC2_("hT", [128, 8 * OWN], BF16)
    hloTg = SC2_("hloTg", [128, 8 * 512], BF16)

    kb.emit("gpsimd", lambda e: e.memset(onesf[:], 1.0))
    kb.emit("gpsimd", lambda e: e.memset(NEG16[:], -1e30), inc="gc")
    cmemev = kb.ev("gc")
    simevs = []
    for dc in range(8):
        simevs.append(kb.dma("sync", lambda e, dc=dc: e.dma_start(out=simf[:, dc*16:(dc+1)*16], in_=sim[dc*128:(dc+1)*128, :]), "sy"))
        kb.dma("gpsimd", lambda e, dc=dc: e.dma_start(out=simhi[:, dc*16:(dc+1)*16], in_=sim[dc*128:(dc+1)*128, :]), "gp")
    sim_ev = kb.qall("sy") + kb.qall("gp")
    slo_ev = None
    for dc in range(8):
        slo_ev = kb.emit("vector", lambda e, dc=dc: e.tensor_tensor(
            simlo[:, dc*16:(dc+1)*16], simf[:, dc*16:(dc+1)*16], simhi[:, dc*16:(dc+1)*16], AL.subtract),
            waits=sim_ev if dc == 0 else (), inc="ve", dr=True)
    csev = None
    for dc in range(8):
        sq = kb.emit("scalar", lambda e, dc=dc: e.activation(simsq[:], simf[:, dc*16:(dc+1)*16], AF.Square),
                     waits=[slo_ev] if dc == 0 else [csev], inc="se")
        csev = kb.emit("tensor", lambda e, dc=dc: e.matmul(psDa[0:16, 0:1], simsq[:], onesf[:],
                       start=(dc == 0), stop=(dc == 7)), waits=[sq], inc="te")
    c1 = kb.emit("scalar", lambda e: e.activation(gsm[0:16, 0:1], psDa[0:16, 0:1], AF.Sqrt), waits=[csev], inc="se")
    c2v = kb.emit("vector", lambda e: e.reciprocal(gsm[0:16, 1:2], gsm[0:16, 0:1]), waits=[c1], inc="ve", dr=True)
    gld = kb.dma("sync", lambda e: e.dma_start(out=gsm[0:1, 8:24], in_=gates), "sy")
    sgev = kb.emit("scalar", lambda e: e.activation(gsm[0:1, 8:24].rearrange("a b -> a b"), gsm[0:1, 8:24], AF.Sigmoid),
                   waits=[gld], inc="se")
    sgmm = kb.emit("tensor", lambda e: e.matmul(psDl[:, 0:16], ones1f[0:1, :], gsm[0:1, 8:24], start=True, stop=True),
                   waits=[sgev, cmemev, csev], inc="te")
    sgcp = kb.emit("vector", lambda e: e.tensor_copy(SIGB[:], psDl[:, 0:16]), waits=[sgmm], inc="ve", dr=True)

    rw_evs = []
    hT_evs = []
    prevT = None
    prev_sqv = None
    inv_prev = None
    x2ld_hist = []
    for t in range(NT):
        g, tt = t // 4, t % 4
        pb = (t % 2) * 32
        buf = xa[t % 2]
        w = [spill_evs[g]] if tt == 0 else []
        if t >= 2: w.append(hT_evs[t - 2][1])
        ldev = kb.dma("sync", lambda e, buf=buf, t=t: e.dma_start(out=buf[:], in_=x2scr[t*128:(t+1)*128, :]), "sy", waits=w)
        x2ld_hist.append(ldev)
        ev3, mu3, r3 = ln_stats(buf, pb, [ldev])
        hev = kb.emit("vector", lambda e, buf=buf, mu3=mu3, r3=r3: e.tensor_scalar(hbuf[:], buf[:], mu3, r3, AL.subtract, AL.mult),
                      waits=[ev3] + ([prev_sqv] if prev_sqv else []), inc="ve", dr=True)
        if "h" in dbg_t:
            kb.dma("gpsimd", lambda e, t=t: e.dma_start(out=dbg_t["h"][t*128:(t+1)*128, :], in_=hbuf[:]), "gp", waits=[hev])
        sqv = kb.emit("scalar", lambda e: e.activation(sqscr[:], hbuf[:], AF.Square, accum_out=gsm[:, 2:3]), waits=[hev], inc="se")
        prev_sqv = sqv
        sdv = kb.emit("scalar", lambda e: e.activation(gsm[:, 3:4], gsm[:, 2:3], AF.Sqrt, bias=1e-12),
                      waits=[sqv] + ([inv_prev] if inv_prev else []), inc="se")
        inv = kb.emit("vector", lambda e, t=t: e.reciprocal(invncol[:, t:t+1], gsm[:, 3:4]), waits=[sdv], inc="ve", dr=True)
        inv_prev = inv
        hbe = kb.emit("vector", lambda e: e.tensor_scalar(hb16[:], hbuf[:], 1.0, None, AL.mult),
                      waits=[hev] + ([prevT] if prevT else []), inc="ve", dr=True)
        hle = kb.emit("vector", lambda e: e.tensor_tensor(hlo16[:], hbuf[:], hb16[:], AL.subtract), waits=[hbe], inc="ve", dr=True)
        tev = None
        cp1 = cp2 = None
        for dc in range(8):
            w = [hbe] if dc == 0 else [cp1]
            t1 = kb.emit("tensor", lambda e, dc=dc: e.transpose(psT2[:, 0:128], hb16[:, dc*128:(dc+1)*128], eyeb[:]),
                         waits=w, inc="te")
            cp1 = kb.emit("scalar", lambda e, dc=dc, t=t: e.copy(hT[:, dc*OWN + t*128: dc*OWN + t*128 + 128], psT2[:, 0:128]),
                          waits=[t1], inc="se")
            t2_ = kb.emit("tensor", lambda e, dc=dc: e.transpose(psT2[:, 1024:1152], hlo16[:, dc*128:(dc+1)*128], eyeb[:]),
                          waits=([hle] if dc == 0 else [cp2]), inc="te")
            cp2 = kb.emit("scalar", lambda e, dc=dc, tt=tt: e.copy(hloTg[:, dc*512 + tt*128: dc*512 + tt*128 + 128], psT2[:, 1024:1152]),
                          waits=[t2_], inc="se")
        tev = cp2
        prevT = cp2
        hT_evs.append((tev, kb.ev("se")))
        if tt == 3:
            gmm = None
            idx = 0
            for dc in range(8):
                for pi, (rh,) in enumerate((("hT",), ("hlo",), ("loT",))):
                    st = (idx == 0); sp = (idx == 23); idx += 1
                    w = [tev, sgcp, c2v, slo_ev] if st else ()
                    if st and rw_evs: w.append(rw_evs[-1])
                    gmm = kb.emit("tensor", lambda e, dc=dc, pi=pi, g=g, st=st, sp=sp: e.matmul(
                        psDa[0:16, :],
                        (simhi if pi < 2 else simlo)[:, dc*16:(dc+1)*16],
                        (hloTg[:, dc*512:(dc+1)*512] if pi == 1 else hT[:, dc*OWN + g*512: dc*OWN + (g+1)*512]),
                        start=st, stop=sp), waits=w, inc="te")
            gle = kb.emit("vector", lambda e: e.tensor_scalar(lgt[:], psDa[0:16, :], gsm[0:16, 1:2], None, AL.mult),
                          waits=[gmm], inc="ve", dr=True)
            prev_gt = None
            for tt2 in range(4):
                t2 = g * 4 + tt2
                w = [gle]
                if prev_gt is not None: w.append(prev_gt)
                trev = kb.emit("tensor", lambda e, tt2=tt2: e.transpose(psDl[:, 0:16], lgt[:, tt2*128:(tt2+1)*128], eyef[0:16, 0:16]),
                               waits=w, inc="te")
                gsc = lambda k: gtile[:, 64+k:65+k]
                Lev = kb.emit("vector", lambda e, t2=t2: e.scalar_tensor_tensor(
                    gtile[:, 0:16], psDl[:, 0:16], invncol[:, t2:t2+1], SIGB[:], AL.mult, AL.subtract),
                    waits=[trev], inc="ve", dr=True)
                prev_gt = Lev
                kb.emit("vector", lambda e: e.tensor_scalar(gtile[:, 16:32], gtile[:, 0:16], 0.0, None, AL.max), dr=True)
                kb.emit("vector", lambda e: e.reduce_max(gsc(0), gtile[:, 16:32], axis=mybir.AxisListType.X), dr=True)
                kb.emit("vector", lambda e: e.tensor_scalar(gsc(1), gsc(0), 0.0, None, AL.is_le), dr=True)
                kb.emit("vector", lambda e: e.reduce_max(gsc(2), gtile[:, 0:16], axis=mybir.AxisListType.X), dr=True)
                kb.emit("vector", lambda e: e.tensor_scalar(gtile[:, 32:48], gtile[:, 0:16], gsc(2), None, AL.is_equal), dr=True)
                kb.emit("vector", lambda e: e.scalar_tensor_tensor(gtile[:, 48:64], gtile[:, 32:48], -1e30, gtile[:, 0:16], AL.mult, AL.add), dr=True)
                kb.emit("vector", lambda e: e.reduce_max(gsc(3), gtile[:, 48:64], axis=mybir.AxisListType.X), dr=True)
                kb.emit("vector", lambda e: e.tensor_scalar(gtile[:, 48:64], gtile[:, 48:64], gsc(3), None, AL.is_equal), dr=True)
                kb.emit("vector", lambda e: e.tensor_tensor(gtile[:, 32:48], gtile[:, 32:48], gtile[:, 48:64], AL.add), dr=True)
                kb.emit("vector", lambda e: e.tensor_scalar(gtile[:, 48:64], gtile[:, 16:32], 0.0, None, AL.is_gt), dr=True)
                kb.emit("vector", lambda e: e.scalar_tensor_tensor(gtile[:, 48:64], gtile[:, 32:48], gsc(1), gtile[:, 48:64], AL.mult, AL.add), dr=True)
                kb.emit("vector", lambda e: e.tensor_tensor(gtile[:, 0:16], gtile[:, 16:32], gtile[:, 48:64], AL.mult), dr=True)
                kb.emit("vector", lambda e: e.tensor_scalar(gtile[:, 48:64], gtile[:, 48:64], 1e30, -1e30, AL.mult, AL.add), dr=True)
                kb.emit("vector", lambda e: e.tensor_tensor(gtile[:, 32:48], gtile[:, 0:16], gtile[:, 48:64], AL.add), dr=True)
                kb.emit("vector", lambda e: e.reduce_max(gsc(4), gtile[:, 32:48], axis=mybir.AxisListType.X), dr=True)
                nrmev = kb.emit("vector", lambda e: e.tensor_scalar(gsc(5), gsc(4), -1.0, None, AL.mult), inc="ve", dr=True)
                eev2 = kb.emit("scalar", lambda e: e.activation(gtile[:, 16:32], gtile[:, 32:48], AF.Exp,
                               bias=gsc(5), accum_out=gsc(6)), waits=[nrmev], inc="se")
                rcpe = kb.emit("vector", lambda e: e.reciprocal(gsc(7), gsc(6)), waits=[eev2], inc="ve", dr=True)
                rwe = kb.emit("vector", lambda e, t2=t2: e.tensor_scalar(rw[:, t2*16:(t2+1)*16], gtile[:, 16:32], gsc(7), None, AL.mult),
                              waits=[rcpe], inc="ve", dr=True)
                prev_gt = rwe
            rw_evs.append(rwe)
    c1_done_ve = kb.ev("ve"); c1_done_se = kb.ev("se"); c1_done_te = kb.ev("te")
    if upto == "c1":
        kb.emit("sync", lambda e: e.nop(), waits=[c1_done_se, c1_done_ve, c1_done_te] + kb.qall("gp") + kb.qall("sy"))
        kb.finalize()
        return es, dbg_t

    wstream = [SC2_("wstream", [128, 8 * 2048], BF16)]
    qkvacc = SC2_("qkvacc", [128, 3 * 16 * 128])
    psP = psD
    psP2 = psP
    wevs = {}
    def load_pg(pg, waits):
        wt = (qp, kp, vp)[pg]
        evs = []
        for dc in range(8):
            evs.append(kb.dma("gpsimd", lambda e, pg=pg, dc=dc, wt=wt: e.dma_start(
                out=wstream[0][:, dc*2048:(dc+1)*2048], in_=wt[dc*128:(dc+1)*128, :]),
                "gp", waits=waits if dc == 0 else ()))
        return evs
    wevs[0] = load_pg(0, [op_done_te])
    mix_evs = []
    pg_mm_done = {}
    for pg in range(3):
        if pg >= 1:
            wevs[pg] = load_pg(pg, [pg_mm_done[pg - 1]])
        wv = wevs[pg]
        for t in range(NT):
            for grp in range(8):
                pair = (pg * 128 + t * 8 + grp) % 3
                accs = ((psD[:, 0:128], psD[:, 512:640]),
                        (psD[:, 1024:1152], psD[:, 1536:1664]),
                        (psDa[:, 0:128], psDl[:, 0:128]))[pair]
                mmev = None
                for dc in range(8):
                    for ee in range(2):
                        e_ = grp * 2 + ee
                        w = ()
                        if dc == 0 and ee == 0:
                            w = list(wv[-6:]) + [rw_evs[t // 4], hT_evs[t][1]]
                            if pair == 2: w.append(rw_evs[3])
                            if len(mix_evs) >= 3: w.append(mix_evs[-3])
                        mmev = kb.emit("tensor", lambda e, dc=dc, ee=ee, e_=e_, t=t, pg=pg, accs=accs: e.matmul(
                            accs[ee],
                            hT[:, dc*OWN + t*128: dc*OWN + (t+1)*128],
                            wstream[0][:, dc*2048 + e_*128: dc*2048 + (e_+1)*128],
                            start=(dc == 0), stop=(dc == 7)), waits=w, inc="te")
                mev2 = None
                for ee in range(2):
                    e_ = grp * 2 + ee
                    if e_ == 0:
                        mev2 = kb.emit("vector", lambda e, ee=ee, t=t, pg=pg, accs=accs: e.tensor_scalar(
                            qkvacc[:, pg*2048 + t*128: pg*2048 + (t+1)*128],
                            accs[ee],
                            rw[:, t*16:t*16+1], None, AL.mult), waits=[mmev], inc="ve", dr=True)
                    else:
                        mev2 = kb.emit("vector", lambda e, ee=ee, e_=e_, t=t, pg=pg, accs=accs: e.scalar_tensor_tensor(
                            qkvacc[:, pg*2048 + t*128: pg*2048 + (t+1)*128],
                            accs[ee],
                            rw[:, t*16+e_:t*16+e_+1], qkvacc[:, pg*2048 + t*128: pg*2048 + (t+1)*128],
                            AL.mult, AL.add), waits=[mmev] if ee == 0 else (), inc="ve", dr=True)
                mix_evs.append(mev2)
        pg_mm_done[pg] = kb.ev("te")
    c2_ve = kb.ev("ve")

    prev3 = None
    c3_evs = []
    for pg in range(3):
        for t in range(NT):
            cev = kb.emit("vector", lambda e, pg=pg, t=t: e.tensor_scalar(
                qkvb[:], qkvacc[:, pg*2048 + t*128: pg*2048 + (t+1)*128], 1.0, None, AL.mult),
                waits=[c2_ve] if (pg == 0 and t == 0) else ([prev3] if prev3 else ()), inc="ve", dr=True)
            trev = kb.emit("tensor", lambda e: e.transpose(psT2[:, 0:128], qkvb[:], eyeb[:]), waits=[cev], inc="te")
            if pg == 0:
                prev3 = kb.emit("scalar", lambda e, t=t: e.copy(qtT[:, t*128:(t+1)*128], psT2[:, 0:128]), waits=[trev], inc="se")
            else:
                prev3 = kb.emit("scalar", lambda e, pg=pg, t=t: e.copy(kvst[:, (pg-1)*OWN + t*128: (pg-1)*OWN + (t+1)*128], psT2[:, 0:128]), waits=[trev], inc="se")
            c3_evs.append(prev3)
    p2evs = []
    for r2 in range(2):
        p2evs.append(kb.dma("sync", lambda e, r2=r2: e.dma_start(
            out=pay2[r2*128:(r2+1)*128, :], in_=kvst[:, r2*OWN:(r2+1)*OWN]), "sy", waits=[c3_evs[-1]]))
    cc3ev = kb.emit("gpsimd", lambda e: e.collective_compute(
        "AllGather", AL.bypass, replica_groups=[list(range(8))], ins=[pay2], outs=[ag2]),
        waits=kb.qall("sy"), inc="cc")

    if upto == "c3":
        kb.emit("sync", lambda e: e.nop(), waits=[cc3ev, kb.ev("se"), kb.ev("ve"), kb.ev("te")] + kb.qall("gp") + kb.qall("sy"))
        kb.finalize()
        return es, dbg_t
    esC2.__exit__(None, None, None)
    SC2 = float(1.0 / np.sqrt(128.0))
    esD = ExitStack(); esD.__enter__()
    SD = lambda name, shape, dt=F32: esD.enter_context(nc.sbuf_tensor(name, shape, dt))
    opb = SD("opb", [128, 16 * D], BF16)
    ktT = SD("ktT", [128, T], BF16)
    nsq2 = SD("nsq2", [128, T], BF16)
    brq16 = SD("brq16", [1, OWN], BF16)
    vtTr = SD("vtTr", [128, T], BF16)
    vtt = SD("vtt", [128, T], BF16)
    E1 = SD("E1", [128, 4 * 512], BF16)
    brq = SD("brq", [1, OWN])
    mld = SD("mld", [128, 128])
    albD = SD("albD", [128, 512])
    mtile = SD("mtile", [128, 512], BF16)
    knm2 = SD("knm2", [128, 8])
    idx3 = SD("idx3", [128, 8], I32)
    idx3_ev = kb.dma("sync", lambda e: e.dma_start(out=idx3[:], in_=idx_kv2), "sy", waits=[cc3ev])
    mld_ev = kb.dma("sync", lambda e: e.dma_start(out=mld[:], in_=mlen), "sy")
    d_g = []
    for t2 in range(2):
        dst = (ktT, vtTr)[t2]
        for g in range(4):
            d_g.append(kb.dma("gpsimd", lambda e, t2=t2, g=g, dst=dst: e.indirect_dma_start(
                out=dst[:, g*1024:(g+1)*1024], out_offset=None,
                in_=ag2v, in_offset=IndirectOffsetOnAxis(idx3[:, t2*4+g:t2*4+g+1], 0)),
                "gp", waits=[idx3_ev, cc3ev] if (t2 == 0 and g == 0) else ()))
    kv2_ev = kb.qall("gp")
    vtev = None
    for kt in range(32):
        vtev = kb.dma("sync", lambda e, kt=kt: e.dma_start(
            out=vtt[:, kt*128:(kt+1)*128], in_=vtTr[:, kt*128:(kt+1)*128], transpose=True),
            "sy", waits=list(kv2_ev) if kt == 0 else ())
    vt_all = kb.qall("sy")
    sqd = kb.emit("scalar", lambda e: e.activation(nsq2[:], ktT[:], AF.Square), waits=list(kv2_ev) + [pg_mm_done[2]], inc="se")
    rdev2 = None
    for rnd in range(2):
        mme = None
        for k2 in range(4):
            cb = rnd * 4 + k2
            w = ([sqd] if rnd == 0 else [rdev2]) if k2 == 0 else ()
            mme = kb.emit("tensor", lambda e, cb=cb, k2=k2: e.matmul(
                psD[:, k2*512:(k2+1)*512], onesb[:], nsq2[:, cb*512:(cb+1)*512],
                start=True, stop=True), waits=w, inc="te")
        rdev2 = kb.emit("vector", lambda e, rnd=rnd: e.reduce_max(
            knm2[:, rnd:rnd+1], psD[:], axis=mybir.AxisListType.X), waits=[mme], inc="ve", dr=True)
    k2ev2 = kb.emit("vector", lambda e: e.reduce_max(knm2[:, 4:5], knm2[:, 0:2], axis=mybir.AxisListType.X), inc="ve", dr=True)
    ksev2 = kb.emit("scalar", lambda e: e.activation(knm2[:, 5:6], knm2[:, 4:5], AF.Sqrt), waits=[k2ev2], inc="se")
    kmev2 = kb.emit("vector", lambda e: e.tensor_scalar(knm2[:, 6:7], knm2[:, 5:6], -1.0, None, AL.mult), waits=[ksev2], inc="ve", dr=True)
    sqd2 = kb.emit("scalar", lambda e: e.activation(nsq2[:, 0:OWN], qtT[:], AF.Square), waits=[mme, c3_evs[15]], inc="se")
    brevs = []
    for cb in range(4):
        bank = cb % 2
        w = [sqd2, kmev2] if cb == 0 else ([brevs[cb - 2]] if cb >= 2 else ())
        mmq = kb.emit("tensor", lambda e, cb=cb, bank=bank: e.matmul(
            psD[:, bank*512:(bank+1)*512], onesb[:], nsq2[:, cb*512:(cb+1)*512],
            start=True, stop=True), waits=w, inc="te")
        sev_ = kb.emit("scalar", lambda e, cb=cb, bank=bank: e.activation(
            brq[0:1, cb*512:(cb+1)*512], psD[0:1, bank*512:(bank+1)*512], AF.Sqrt), waits=[mmq], inc="se")
        brevs.append(sev_)
    brf = kb.emit("vector", lambda e: e.tensor_scalar(brq[0:1, :], brq[0:1, :], knm2[0:1, 6:7], 60.0 / SC2, AL.mult, AL.add),
                  waits=[brevs[-1]], inc="ve", dr=True)
    brb = kb.emit("vector", lambda e: e.tensor_scalar(brq16[0:1, :], brq[0:1, :], 1.0, None, AL.mult), inc="ve", dr=True)
    qbs_slots = D_SLOTS
    exp2 = {}; cons2 = {}
    aost_prev = None
    mev3s = {}
    for sp in range(4):
        nkt = D_SLOTS[sp]
        avev2 = None
        def emit_avd(kt, nkt=nkt):
            esl = (kt % 4) * 512
            st = (kt == 0); spp = (kt == nkt - 1)
            kb.emit("tensor", lambda e, kt=kt, esl=esl, st=st, spp=spp: e.matmul(
                psDa[:], vtt[:, kt*128:(kt+1)*128], E1[:, esl:esl+512], start=st, stop=spp),
                waits=[mev3s[kt]], inc="te")
            av = kb.emit("tensor", lambda e, kt=kt, esl=esl, st=st, spp=spp: e.matmul(
                psDl[:], onesb[:], E1[:, esl:esl+512], start=st, stop=spp), inc="te")
            cons2[kt] = av
            return av
        for kt in range(nkt):
            scol = (kt % 2) * 512
            esl = (kt % 4) * 512
            w = []
            if kt >= 2: w.append(exp2[kt - 2])
            if kt == 0:
                w += [brb] + list(vt_all)
                if aost_prev is not None: w.append(aost_prev)
            mmev = kb.emit("tensor", lambda e, kt=kt, sp=sp, scol=scol: e.matmul(
                psD[:, scol:scol+512], ktT[:, kt*128:(kt+1)*128], qtT[:, sp*512:(sp+1)*512],
                start=True, stop=False), waits=w, inc="te")
            mmev = kb.emit("tensor", lambda e, kt=kt, sp=sp, scol=scol: e.matmul(
                psD[:, scol:scol+512], ones1b[0:1, :], brq16[0:1, sp*512:(sp+1)*512],
                start=False, stop=True), inc="te")
            eev3 = kb.emit("scalar", lambda e, scol=scol, esl=esl: e.activation(
                E1[:, esl:esl+512], psD[:, scol:scol+512], AF.Exp, scale=SC2),
                waits=[mmev] + ([cons2[kt - 4]] if kt >= 4 else []), inc="se")
            exp2[kt] = eev3
            mev3 = kb.emit("vector", lambda e, sp=sp, kt=kt: e.tensor_scalar(
                mtile[:], offsD[:], mld[:, sp*32+kt:sp*32+kt+1], None, AL.is_lt),
                waits=[eev3, mld_ev], inc="ve", dr=True)
            mev3 = kb.emit("vector", lambda e, esl=esl: e.tensor_tensor(
                E1[:, esl:esl+512], E1[:, esl:esl+512], mtile[:], AL.mult),
                waits=[mev3], inc="ve", dr=True)
            mev3s[kt] = mev3
            if kt >= 2:
                avev2 = emit_avd(kt - 2)
        if nkt >= 2:
            avev2 = emit_avd(nkt - 2)
        avev2 = emit_avd(nkt - 1)
        rev2 = kb.emit("vector", lambda e: e.tensor_scalar(albD[:], psDl[:], 1.2e-38, None, AL.max), waits=[avev2], inc="ve", dr=True)
        rev2 = kb.emit("vector", lambda e: e.reciprocal(albD[:], albD[:]), waits=[rev2], inc="ve", dr=True)
        aost_prev = kb.emit("vector", lambda e, sp=sp: e.tensor_tensor(aoT[:, sp*512:(sp+1)*512], psDa[:], albD[:], AL.mult),
                            waits=[rev2], inc="ve", dr=True)
    d_done = kb.ev("ve")
    if "ao" in dbg_t:
        kb.dma("gpsimd", lambda e: e.dma_start(out=dbg_t["ao"][:, 0:OWN], in_=aoT[:]), "gp", waits=[d_done])

    if upto == "d":
        kb.emit("sync", lambda e: e.nop(), waits=[d_done, kb.ev("se"), kb.ev("te")] + kb.qall("gp") + kb.qall("sy"))
        kb.finalize()
        return es, dbg_t
    og = x2g
    for qq in range(4):
        kb.dma("gpsimd", lambda e, qq=qq: e.dma_start(out=opb[:, qq*4096:(qq+1)*4096],
               in_=opw[:, qq*4096:(qq+1)*4096]), "gp", waits=[pg_mm_done[2]] if qq == 0 else ())
    opb_ev = kb.qall("gp")
    rwbe = kb.emit("vector", lambda e: e.tensor_scalar(rwb[:], rw[:], 1.0, None, AL.mult), waits=[c2_ve, d_done], inc="ve", dr=True)
    prev4 = None
    for t in range(NT):
        trev = kb.emit("tensor", lambda e, t=t: e.transpose(psT2[0:16, 0:128], rwb[:, t*16:(t+1)*16], eyeb[:]),
                       waits=[rwbe] if t == 0 else [prev4], inc="te")
        prev4 = kb.emit("scalar", lambda e, t=t: e.copy(rwT[0:16, t*128:(t+1)*128], psT2[0:16, 0:128]), waits=[trev], inc="se")
    rwT_ev = prev4
    out_evs = []
    og_tt = [None, None, None, None]
    og_guard = None
    for g in range(4):
        xev4 = []
        for tt in range(4):
            t = g * 4 + tt
            xev4.append(kb.dma("sync", lambda e, t=t, tt=tt: e.dma_start(out=xrl[tt][:], in_=x2scr[t*128:(t+1)*128, :]), "sy",
                               waits=[d_done] if (g == 0 and tt == 0) else ()))
        aeev = None
        for e_ in range(16):
            w = ([rwT_ev, d_done] + (list(opb_ev) if g == 0 else [])) if e_ == 0 else [aeev]
            rbm = kb.emit("tensor", lambda e, e_=e_, g=g: e.matmul(
                psDl[:], oh16[:, e_*128:(e_+1)*128], rwT[0:16, g*512:(g+1)*512], start=True, stop=True),
                waits=w, inc="te")
            aeev = kb.emit("vector", lambda e, e_=e_, g=g: e.tensor_tensor(
                aoe[:, e_*512:(e_+1)*512], psDl[:], aoT[:, g*512:(g+1)*512], AL.mult),
                waits=[rbm], inc="ve", dr=True)
        prev_tr = None
        addevs = []
        for half in range(2):
            mmv = None
            for e_ in range(16):
                for oc in range(4):
                    occ = half * 4 + oc
                    w = ()
                    if e_ == 0 and oc == 0:
                        w = [aeev, kb.ev("se")]
                    mmv = kb.emit("tensor", lambda e, e_=e_, occ=occ, oc=oc: e.matmul(
                        psD[:, oc*512:(oc+1)*512], opb[:, e_*D + occ*128: e_*D + (occ+1)*128],
                        aoe[:, e_*512:(e_+1)*512], start=(e_ == 0), stop=(e_ == 15)), waits=w, inc="te")
            for oc in range(4):
                occ = half * 4 + oc
                scv = kb.emit("scalar", lambda e, oc=oc: e.copy(ystg[oc % 2][:], psD[:, oc*512:(oc+1)*512]),
                              waits=[mmv] + ([prev_tr] if prev_tr else []), inc="se")
                trl = []
                for tt in range(4):
                    trl.append(kb.emit("tensor", lambda e, tt=tt, oc=oc: e.transpose(
                        psT2[:, (oc % 2)*1024 + tt*256: (oc % 2)*1024 + tt*256 + 128],
                        ystg[oc % 2][:, tt*128:(tt+1)*128], eyeb[:]), waits=[scv], inc="te"))
                lastadd = None
                for tt in range(4):
                    w2 = [trl[tt], xev4[tt]]
                    if half == 0 and oc == 0 and og_tt[tt] is not None:
                        w2.append(og_tt[tt])
                    lastadd = kb.emit("vector", lambda e, tt=tt, occ=occ, oc=oc: e.tensor_tensor(
                        og[:, tt*D + occ*128: tt*D + (occ+1)*128],
                        psT2[:, (oc % 2)*1024 + tt*256: (oc % 2)*1024 + tt*256 + 128],
                        xrl[tt][:, occ*128:(occ+1)*128], AL.add), waits=[trl[3]] + w2, inc="ve")
                prev_tr = lastadd
                addevs.append(lastadd)
        for tt in range(4):
            t = g * 4 + tt
            ev_ = kb.dma("sync", lambda e, t=t, tt=tt: e.dma_start(
                out=out_own[t*128:(t+1)*128, :], in_=og[:, tt*D:(tt+1)*D]), "sy", waits=[addevs[-1]])
            out_evs.append(ev_)
            og_tt[tt] = ev_

    kb.emit("sync", lambda e: e.nop(), waits=[kb.ev("se"), kb.ev("ve"), kb.ev("te")] + kb.qall("gp") + kb.qall("sy"))
    pre = []
    for v, i in cvals.items():
        pre.append(((lambda e, v=v, i=i: e.memset(cpool[:, i:i+1], v)), [], []))
    kb.streams["gpsimd"] = pre + kb.streams["gpsimd"]
    kb.finalize()
    return es, dbg_t


_CACHE = {}

def kernel(**inputs):
    per_core = host_prep(inputs)
    if "nc" not in _CACHE:
        nc = bass.Bass("TRN2", target_bir_lowering=False)
        build(nc)
        _CACHE["nc"] = nc
    nc = _CACHE["nc"]
    in_maps = []
    for c in range(8):
        d = per_core[c]
        m = {k: np.ascontiguousarray(v) for k, v in d.items() if k != "op"}
        m["opw"] = np.ascontiguousarray(d["op"])
        in_maps.append(m)
    res = run_bass_kernel_spmd(nc, in_maps, core_ids=list(range(8)))
    return assemble_out(res.results)


# version 15 (final)
# speedup vs baseline: 1.1706x; 1.1706x over previous
import numpy as np
from contextlib import ExitStack
import concourse.bass as bass
import concourse.mybir as mybir
from concourse.bass_utils import run_bass_kernel_spmd
from concourse.bass_core_rust import IndirectOffsetOnAxis

F32 = mybir.dt.float32
BF16 = mybir.dt.bfloat16
I32 = mybir.dt.int32
AF = mybir.ActivationFunctionType
AL = mybir.AluOpType

ENGINES = ("tensor", "vector", "scalar", "sync", "gpsimd")

NPOOL = 8

class KB:
    def __init__(self, nc, es):
        self.nc = nc
        self.es = es
        self.streams = {e: [] for e in ENGINES}
        self.sems = {}
        self.cnt = {}
        for s in ("te", "ve", "se", "gc", "cc"):
            self.sems[s] = es.enter_context(nc.semaphore(s))
            self.cnt[s] = 0
        self.pools = {}
        self.qctr = {}
        self.qhist = {}
        for q in ("sy", "gp"):
            self.pools[q] = []
            for i in range(NPOOL):
                s = f"{q}{i}"
                self.sems[s] = es.enter_context(nc.semaphore(s))
                self.cnt[s] = 0
                self.pools[q].append(s)
            self.qctr[q] = 0
            self.qhist[q] = []

    def dma(self, eng, fn, queue, waits=()):
        i = self.qctr[queue]
        s = self.pools[queue][i % NPOOL]
        w = [x for x in waits if x is not None]
        if i >= NPOOL:
            w.append(self.qhist[queue][i - NPOOL])
        self.cnt[s] += 16
        ev = (s, self.cnt[s])
        self.streams[eng].append((fn, w, [(s, 16)]))
        self.qhist[queue].append(ev)
        self.qctr[queue] = i + 1
        return ev

    def qall(self, queue):
        h = self.qhist[queue]
        return h[-NPOOL:] if h else []

    def emit(self, eng, fn, waits=(), inc=None, n=1, dr=False):
        incs = []
        if inc is not None:
            self.cnt[inc] += n
            incs = [(inc, n)]
        if dr:
            fn0 = fn
            fn = (lambda e, fn0=fn0: (e.drain(), fn0(e))[1])
        self.streams[eng].append((fn, [w for w in waits if w is not None], incs))
        return (inc, self.cnt[inc]) if inc is not None else None

    def ev(self, s):
        return (s, self.cnt[s])

    def finalize(self):
        block = self.es.enter_context(self.nc.Block())
        def mk(engname):
            def body(eng):
                lw = {}
                for fn, waits, incs in self.streams[engname]:
                    for (s, v) in waits:
                        if v > lw.get(s, 0):
                            eng.wait_ge(self.sems[s], v)
                            lw[s] = v
                    ins = fn(eng)
                    for (s, n) in incs:
                        assert ins is not None, f"{engname} inc without instruction"
                        ins.then_inc(self.sems[s], n)
            return body
        block.sync(mk("sync"))
        block.gpsimd(mk("gpsimd"))
        block.tensor(mk("tensor"))
        block.vector(mk("vector"))
        block.scalar(mk("scalar"))


HB, HT, HD_, HH, HDH, HNE, HHD2 = 4, 4096, 1024, 16, 64, 16, 128
HNC = 8
OWN = 2048
NT = 16
NG = 4
EPS = 1e-5

def own_chunks(c): return (0, 3) if c % 2 == 0 else (1, 2)
def own_qblks(c):  return (0, 1, 6, 7) if c % 2 == 0 else (2, 3, 4, 5)
D_SLOTS = (12, 16, 28, 32)
def d_slot_qblks(c):
    qb = sorted(own_qblks(c), key=lambda q: q)
    return qb

def host_prep(inputs):
    x = np.asarray(inputs["L0_attn_x"], dtype=np.float32)
    Wqkv = np.asarray(inputs["L0_attn_W_qkv"], dtype=np.float32)
    Wo = np.asarray(inputs["L0_attn_W_o"], dtype=np.float32)
    gamma = np.asarray(inputs["L0_attn_ln_gamma"], dtype=np.float32)
    beta = np.asarray(inputs["L0_attn_ln_beta"], dtype=np.float32)
    sim = np.asarray(inputs["L0_ffn_sim_matrix"], dtype=np.float32)
    gates = np.asarray(inputs["L0_ffn_gates"], dtype=np.float32)
    qp = np.asarray(inputs["L0_ffn_q_proj"], dtype=np.float32)
    kp = np.asarray(inputs["L0_ffn_k_proj"], dtype=np.float32)
    vp = np.asarray(inputs["L0_ffn_v_proj"], dtype=np.float32)
    op = np.asarray(inputs["L0_ffn_o_proj"], dtype=np.float32)

    perm = []
    for t in range(3):
        for hp in range(8):
            for hh in range(2):
                h = 2 * hp + hh
                base = h * 192 + t * 64
                perm.extend(range(base, base + 64))
    WqkvT = np.ascontiguousarray(Wqkv[perm].T)
    WoT = np.ascontiguousarray(Wo.T)
    g8 = np.ascontiguousarray(gamma.reshape(8, 128).T)
    b8 = np.ascontiguousarray(beta.reshape(8, 128).T)

    qpr = np.ascontiguousarray(qp.transpose(1, 0, 2).reshape(1024, 16 * 128))
    kpr = np.ascontiguousarray(kp.transpose(1, 0, 2).reshape(1024, 16 * 128))
    vpr = np.ascontiguousarray(vp.transpose(1, 0, 2).reshape(1024, 16 * 128))
    opr = np.ascontiguousarray(op.transpose(1, 0, 2).reshape(128, 16 * 1024))
    per_core = []
    for c in range(HNC):
        b = c // 2
        ch = own_chunks(c)
        rows = np.concatenate([np.arange(ch[0]*1024, ch[0]*1024+1024),
                               np.arange(ch[1]*1024, ch[1]*1024+1024)])
        x_own = np.ascontiguousarray(x[b][rows])

        def chunk_rank_j(g):
            r_even, r_odd = 2*b, 2*b+1
            if g in (0, 3): return (r_even, 0 if g == 0 else 1)
            else:           return (r_odd, 0 if g == 1 else 1)
        hp_loc = [c % 2 * 4 + i for i in range(4)]
        idx_qkv = np.zeros((128, 80), dtype=np.int32)
        for i in range(4):
            hpg = hp_loc[i]
            for t in range(2):
                for h in range(2):
                    for g in range(4):
                        r, j = chunk_rank_j(g)
                        d0 = t * 1024 + hpg * 128 + h * 64
                        cc = ((i * 2 + t) * 2 + h) * 4 + g
                        idx_qkv[0:64, cc] = r * 3072 + d0 + np.arange(64)
            for g in range(4):
                r, j = chunk_rank_j(g)
                d0 = 2 * 1024 + hpg * 128
                idx_qkv[:, 64 + i * 4 + g] = r * 3072 + d0 + np.arange(128)
        idx_ctx = np.zeros((128, 16), dtype=np.int32)
        for dc in range(8):
            dims = dc * 128 + np.arange(128)
            rr = 2*b + (dims >= 512)
            for s in range(2):
                g = ch[s]
                idx_ctx[:, dc * 2 + s] = (rr * 128 + np.arange(128)) * 4 + g
        idx_kv2 = np.zeros((128, 8), dtype=np.int32)
        for t in range(2):
            for g in range(4):
                r, j = chunk_rank_j(g)
                idx_kv2[:, t * 4 + g] = (r * 256 + t * 128 + np.arange(128)) * 2 + j
        mlen = np.zeros((128, 128), dtype=np.float32)
        qbs = d_slot_qblks(c)
        for sp in range(4):
            gq = qbs[sp]
            for kt in range(32):
                mlen[:, sp * 32 + kt] = gq * 512 - kt * 128 - np.arange(128) + 1
        per_core.append(dict(
            x_own=x_own, idx_qkv=idx_qkv, idx_ctx=idx_ctx, idx_kv2=idx_kv2, mlen=mlen,
            WqkvT=WqkvT, WoT=WoT, g8=g8, b8=b8, sim=sim, gates=gates.reshape(1, HNE),
            qp=qpr, kp=kpr, vp=vpr, op=opr,
        ))
    return per_core

def assemble_out(results):
    out = np.zeros((HB, HT, HD_), dtype=np.float32)
    for c in range(HNC):
        b = c // 2
        ch = own_chunks(c)
        r = results[c]["out_own"]
        out[b, ch[0]*1024:(ch[0]+1)*1024] = r[0:1024]
        out[b, ch[1]*1024:(ch[1]+1)*1024] = r[1024:2048]
    return out


Bb, T, D, H, DH, NE, HD2 = 4, 4096, 1024, 16, 64, 16, 128
OWN, NT, NG = 2048, 16, 4
EPS = 1e-5
D_SLOTS = (12, 16, 28, 32)

def build(nc, dbg=(), upto='full'):
    es = ExitStack(); es.__enter__()
    kb = KB(nc, es)
    S = lambda name, shape, dt=F32: es.enter_context(nc.sbuf_tensor(name, shape, dt))
    PS = lambda name, shape, dt=F32: es.enter_context(nc.psum_tensor(name, shape, dt))

    dr_in = lambda n, sh, dt=F32: nc.dram_tensor(n, sh, dt, kind="ExternalInput").ap()
    x_own = dr_in("x_own", [OWN, D])
    WqkvT = dr_in("WqkvT", [D, 3 * D])
    WoT = dr_in("WoT", [D, D])
    g8 = dr_in("g8", [128, 8]); b8 = dr_in("b8", [128, 8])
    sim = dr_in("sim", [D, NE]); gates = dr_in("gates", [1, NE])
    qp = dr_in("qp", [D, NE * HD2]); kp = dr_in("kp", [D, NE * HD2]); vp = dr_in("vp", [D, NE * HD2])
    opw = dr_in("opw", [HD2, NE * D])
    idx_qkv = dr_in("idx_qkv", [128, 80], I32)
    idx_ctx = dr_in("idx_ctx", [128, 16], I32)
    idx_kv2 = dr_in("idx_kv2", [128, 8], I32)
    mlen = dr_in("mlen", [128, 128])
    out_own = nc.dram_tensor("out_own", [OWN, D], F32, kind="ExternalOutput").ap()

    pay1q = [nc.dram_tensor(f"pay1q{k}", [3 * D, 512], BF16, kind="Internal").ap() for k in range(4)]
    ag1q = [nc.dram_tensor(f"ag1q{k}", [8 * 3 * D, 512], BF16, kind="Internal", addr_space="Shared").ap() for k in range(4)]
    payc = nc.dram_tensor("payc", [512, T], BF16, kind="Internal").ap()
    agcs = [nc.dram_tensor(f"agc{i}", [8 * 128, T], BF16, kind="Internal", addr_space="Shared").ap() for i in range(4)]
    pay2 = nc.dram_tensor("pay2", [2 * HD2, OWN], BF16, kind="Internal").ap()
    ag2 = nc.dram_tensor("ag2", [8 * 2 * HD2, OWN], BF16, kind="Internal", addr_space="Shared").ap()
    x2scr = nc.dram_tensor("x2scr", [OWN, D], F32, kind="Internal").ap()
    agcvs = [a.rearrange("r (j t) -> (r j) t", j=4) for a in agcs]
    ag2v = ag2.rearrange("r (j t) -> (r j) t", j=2)

    dbg_t = {}
    dbg_specs = {"xnT": ([128, 8 * OWN], BF16), "pay1": ([3 * D, OWN], BF16),
                 "qt": ([3 * 128, T], BF16), "eb": ([128, 4096], BF16),
                 "ctx": ([512, T], BF16), "x2": ([OWN, D], F32), "h": ([OWN, D], F32),
                 "rw": ([OWN, NE], F32), "qkv2": ([OWN, 3 * HD2], F32),
                 "ao": ([HD2, OWN], BF16), "cb": ([1, 16], F32)}
    for k in dbg:
        sh, dt = dbg_specs[k]
        dbg_t[k] = nc.dram_tensor("d_" + k, sh, dt, kind="ExternalOutput").ap()

    cpool = S("cpool", [128, 32], F32)
    cvals = {}
    def constf(v):
        v = float(v)
        if v not in cvals:
            i = len(cvals)
            assert i < 32
            cvals[v] = i
            nc.const_aps.register(F32, v, cpool[:, i:i+1])
        return v

    eyeb = S("eyeb", [128, 128], BF16)
    eyef = S("eyef", [128, 128], F32)
    onesb = S("onesb", [128, 128], BF16)
    ones1f = S("ones1f", [1, 128], F32)
    ones1b = S("ones1b", [1, 128], BF16)
    coliota = S("coliota", [128, 512], F32)
    rowv = S("rowv", [128, 1], F32)
    maskB = S("maskB", [128, 4 * 1024], BF16)
    offsD = S("offsD", [128, 512], F32)
    oh16 = S("oh16", [16, 16 * 128], BF16)
    rv16 = S("rv16", [16, 1], F32)

    kb.emit("gpsimd", lambda e: e.iota(coliota[:], [[1, 512]], base=0, channel_multiplier=0, allow_small_or_imprecise_dtypes=True))
    kb.emit("gpsimd", lambda e: e.iota(rowv[:], [[0, 1]], base=0, channel_multiplier=1, allow_small_or_imprecise_dtypes=True))
    kb.emit("gpsimd", lambda e: e.iota(offsD[:], [[-1, 512]], base=0, channel_multiplier=0, allow_small_or_imprecise_dtypes=True))
    kb.emit("gpsimd", lambda e: e.iota(oh16[:], [[1, 16], [0, 128]], base=0, channel_multiplier=0, allow_small_or_imprecise_dtypes=True))
    kb.emit("gpsimd", lambda e: e.iota(rv16[:], [[0, 1]], base=0, channel_multiplier=1, allow_small_or_imprecise_dtypes=True))
    kb.emit("gpsimd", lambda e: e.memset(onesb[:], 1.0))
    kb.emit("gpsimd", lambda e: e.memset(ones1f[:], 1.0))
    kb.emit("gpsimd", lambda e: e.memset(ones1b[:], 1.0), inc="gc")
    gcev = kb.ev("gc")
    constf(EPS)
    constf(1e-12)
    def emit_cpool():
        pass
    kb.emit("vector", lambda e: e.tensor_scalar(eyef[:], coliota[:, 0:128], rowv[:], None, AL.is_equal), waits=[gcev])
    kb.emit("vector", lambda e: e.tensor_scalar(oh16[:], oh16[:], rv16[:], None, AL.is_equal), dr=True)
    kb.emit("vector", lambda e: e.tensor_copy(eyeb[:], eyef[:]), dr=True)
    for j in range(4):
        kb.emit("vector", lambda e, j=j: e.tensor_scalar(maskB[:, j*1024:j*1024+512], coliota[:], rowv[:], float(128*j), AL.subtract, AL.is_ge))
        kb.emit("vector", lambda e, j=j: e.tensor_scalar(maskB[:, j*1024+512:(j+1)*1024], coliota[:], rowv[:], float(128*j), AL.subtract, AL.is_ge))
    consts_ev = kb.emit("vector", lambda e: e.tensor_copy(eyeb[:], eyef[:]), inc="ve", dr=True)

    kb.emit("scalar", lambda e: e.nop(), waits=[gcev])
    kb.emit("vector", lambda e: e.nop(), waits=[gcev])
    xa = [S(f"xa{i}", [128, D]) for i in range(2)]
    ya = S("ya", [128, D])
    sqscr = S("sqscr", [128, D], BF16)
    xnp = S("xnp", [128, D], BF16)
    smalls = S("smalls", [128, 64])
    esA = ExitStack(); esA.__enter__()
    SA = lambda name, shape, dt=F32: esA.enter_context(nc.sbuf_tensor(name, shape, dt))
    PSA = lambda name, shape, dt=F32: esA.enter_context(nc.psum_tensor(name, shape, dt))
    Wqkvb = SA("Wqkvb", [128, 8 * 3 * D], BF16)
    for dc in range(8):
        kb.dma("gpsimd", lambda e, dc=dc: e.dma_start(out=Wqkvb[:, dc*3072:(dc+1)*3072], in_=WqkvT[dc*128:(dc+1)*128, :]), "gp")
    wqkv_ev = kb.qall("gp")
    g8b = SA("g8b", [128, 8]); b8b = SA("b8b", [128, 8])
    gb_ev0 = kb.dma("sync", lambda e: e.dma_start(out=g8b[:], in_=g8), "sy")
    gb_ev = kb.dma("sync", lambda e: e.dma_start(out=b8b[:], in_=b8), "sy")

    xnT = SA("xnT", [128, 8 * 512], BF16)
    psT = [PSA(f"psT{i}", [128, 512], BF16) for i in range(2)]
    qkps = [PSA(f"qkps{i}", [128, 512]) for i in range(2)]
    qstage = [SA(f"qstage{i}", [128, 512], BF16) for i in range(2)]
    col = lambda tns, i: tns[:, i:i+1]

    def ln_stats(src, base, waitv):
        bns = smalls[:, base:base+12]
        mu = col(smalls, base+12)
        var = col(smalls, base+13)
        sd = col(smalls, base+14)
        r = col(smalls, base+15)
        kb.emit("vector", lambda e: e.bn_stats(bns[:, 0:6], src[:, 0:512]), waits=waitv, dr=True)
        kb.emit("vector", lambda e: e.bn_stats(bns[:, 6:12], src[:, 512:1024]), dr=True)
        vev = kb.emit("vector", lambda e: e.bn_aggr(smalls[:, base+12:base+14], bns[:]), inc="ve", dr=True)
        sdev = kb.emit("scalar", lambda e: e.activation(sd, var, AF.Sqrt, bias=EPS), waits=[vev], inc="se")
        vev2 = kb.emit("vector", lambda e: e.reciprocal(r, sd), waits=[sdev], inc="ve")
        return vev2, mu, r

    pay_evs = []
    ccq = []
    dump_gate = []
    prev_evac = None
    yev_hist = []
    for t in range(NT):
        g, tt = t // 4, t % 4
        pb = (t % 2) * 32
        buf = xa[t % 2]
        ldw = [consts_ev]
        if t >= 2: ldw.append(yev_hist[t - 2])
        ldev = kb.dma("sync", lambda e, buf=buf, t=t: e.dma_start(out=buf[:], in_=x_own[t*128:(t+1)*128, :]), "sy", waits=ldw)
        ev1, mu1, r1 = ln_stats(buf, pb, [ldev])
        yev = kb.emit("vector", lambda e, buf=buf, mu1=mu1, r1=r1: e.tensor_scalar(ya[:], buf[:], mu1, r1, AL.subtract, AL.mult), waits=[ev1], inc="ve", dr=True)
        yev_hist.append(yev)
        ev2, mu2, r2 = ln_stats(ya, pb + 16, [yev])
        xev = kb.emit("vector", lambda e, mu2=mu2, r2=r2: e.tensor_scalar(xnp[:], ya[:], mu2, r2, AL.subtract, AL.mult), waits=[ev2], inc="ve", dr=True)
        tev = None
        for dc in range(8):
            bank, jj = psT[dc // 4], (dc % 4) * 128
            w = ([xev] + ([prev_evac] if prev_evac else [])) if dc == 0 else ()
            tev = kb.emit("tensor", lambda e, dc=dc, bank=bank, jj=jj: e.transpose(bank[:, jj:jj+128], xnp[:, dc*128:(dc+1)*128], eyeb[:]), waits=w, inc="te")
        vev = None
        for dc in range(8):
            bank, jj = psT[dc // 4], (dc % 4) * 128
            w = ([tev, gb_ev, gb_ev0] + dump_gate) if dc == 0 else ()
            vev = kb.emit("vector", lambda e, dc=dc, bank=bank, jj=jj, tt=tt: e.tensor_scalar(
                xnT[:, dc*512 + tt*128: dc*512 + tt*128 + 128], bank[:, jj:jj+128],
                g8b[:, dc:dc+1], b8b[:, dc:dc+1], AL.mult, AL.add), waits=w, inc="ve")
        prev_evac = vev
        if "xnT" in dbg_t:
            for dc in range(8):
                kb.dma("gpsimd", lambda e, dc=dc, t=t, tt=tt: e.dma_start(
                    out=dbg_t["xnT"][:, dc*2048 + t*128: dc*2048 + t*128 + 128],
                    in_=xnT[:, dc*512 + tt*128: dc*512 + tt*128 + 128]), "gp", waits=[vev])
            if tt == 3:
                dump_gate = kb.qall("gp")
        if tt == 3:
            grp_ev = vev
            for oc in range(24):
                ps, stg = qkps[oc % 2], qstage[oc % 2]
                mmev = None
                for dc in range(8):
                    w = ()
                    if dc == 0:
                        w = [grp_ev] + list(wqkv_ev) + ([pay_evs[-2][1]] if len(pay_evs) >= 2 else [])
                    mmev = kb.emit("tensor", lambda e, oc=oc, dc=dc, ps=ps: e.matmul(
                        ps[:], Wqkvb[:, dc*3072 + oc*128: dc*3072 + oc*128 + 128],
                        xnT[:, dc*512:(dc+1)*512], start=(dc == 0), stop=(dc == 7)), waits=w, inc="te")
                scev = kb.emit("scalar", lambda e, ps=ps, stg=stg: e.copy(stg[:], ps[:]),
                               waits=[mmev] + ([pay_evs[-2][0]] if len(pay_evs) >= 2 else []), inc="se")
                dmev = kb.dma("sync", lambda e, oc=oc, stg=stg, pdst=pay1q[g]: e.dma_start(
                    out=pdst[oc*128:(oc+1)*128, :], in_=stg[:]), "sy", waits=[scev])
                pay_evs.append((dmev, scev))

            ccq.append(kb.emit("gpsimd", lambda e, g=g: e.collective_compute(
                "AllGather", AL.bypass, replica_groups=[list(range(8))], ins=[pay1q[g]], outs=[ag1q[g]]),
                waits=kb.qall("sy"), inc="cc"))
            prev_evac = kb.ev("te")

    ccev = ccq[-1]

    esA.__exit__(None, None, None)
    a_done_te = kb.ev("te")
    esB = ExitStack(); esB.__enter__()
    S_ = lambda name, shape, dt=F32: esB.enter_context(nc.sbuf_tensor(name, shape, dt))
    PS_ = lambda name, shape, dt=F32: esB.enter_context(nc.psum_tensor(name, shape, dt))
    QTa = [S_(f"QTa{i}", [65, 2 * T], BF16) for i in range(2)]
    K2Ta = [S_(f"K2Ta{i}", [65, 2 * T], BF16) for i in range(2)]
    VTr = [S_(f"VTr{i}", [128, T], BF16) for i in range(2)]
    Vt = [S_(f"Vt{i}", [128, T], BF16) for i in range(2)]
    Ebuf = S_("Ebuf", [128, 4 * 1024], BF16)
    nsq = S_("nsq", [64, 2 * T], BF16)
    knm = S_("knm", [128, 8])
    ctxst = [S_(f"ctxst{h}", [64, 512], BF16) for h in range(2)]
    alb2 = S_("alb2", [64, 512])
    psSS = PS_("psSS", [128, 2048])
    psC = [PS_(f"psC{h}", [128, 512]) for h in range(2)]
    psL2 = [PS_(f"psL{h}", [128, 512]) for h in range(2)]

    idxs = S_("idxs", [128, 80], I32)
    idxs_ev = kb.dma("sync", lambda e: e.dma_start(out=idxs[:], in_=idx_qkv), "sy", waits=[pay_evs[-1][0], a_done_te])
    kb.emit("gpsimd", lambda e: e.memset(K2Ta[0][64:65, :], 1.0), inc="gc")
    kb.emit("gpsimd", lambda e: e.memset(K2Ta[1][64:65, :], 1.0), inc="gc")
    ones_ev = kb.ev("gc")

    J4 = (0, 1, 0, 1, 2, 3, 2, 3)
    def load_hp(i, buf, waits):
        firstq = [True] * 4
        def gw(j4):
            if firstq[j4]:
                firstq[j4] = False
                return [idxs_ev, ccq[j4]] + list(waits)
            return []
        for t3 in range(2):
            dst = (QTa, K2Ta)[t3][buf]
            for h in range(2):
                for g in range(4):
                    cc = ((i * 2 + t3) * 2 + h) * 4 + g
                    for hf in range(2):
                        b8 = g * 2 + hf
                        j4 = J4[b8]
                        kb.dma("gpsimd", lambda e, cc=cc, dst=dst, h=h, g=g, hf=hf, j4=j4: e.indirect_dma_start(
                            out=dst[0:64, h*T + g*1024 + hf*512: h*T + g*1024 + (hf+1)*512], out_offset=None,
                            in_=ag1q[j4], in_offset=IndirectOffsetOnAxis(idxs[0:64, cc:cc+1], 0)), "gp", waits=gw(j4))
        for g in range(4):
            cc = 64 + i * 4 + g
            for hf in range(2):
                b8 = g * 2 + hf
                j4 = J4[b8]
                kb.dma("gpsimd", lambda e, cc=cc, buf=buf, g=g, hf=hf, j4=j4: e.indirect_dma_start(
                    out=VTr[buf][:, g*1024 + hf*512: g*1024 + (hf+1)*512], out_offset=None,
                    in_=ag1q[j4], in_offset=IndirectOffsetOnAxis(idxs[:, cc:cc+1], 0)), "gp", waits=gw(j4))
        return kb.qall("gp")

    cc2s = []
    ldB = [None, None]
    ldB[0] = load_hp(0, 0, [])
    hp_done = []
    ctxdma_evs = []
    hp_prev_sclear = None
    for i in range(4):
        buf = i % 2
        ld_ev = ldB[buf]
        if i + 1 < 4:
            w = [hp_done[i - 1]] if i >= 1 else []
            ldB[(i + 1) % 2] = load_hp(i + 1, (i + 1) % 2, w)
        vt_last = None
        for kt in range(32):
            w = list(ld_ev) if kt == 0 else ()
            if i >= 2 and kt == 0:
                w = list(ld_ev) + [hp_done[i - 2]]
            vt_last = kb.dma("sync", lambda e, kt=kt, buf=buf: e.dma_start(
                out=Vt[buf][:, kt*128:(kt+1)*128], in_=VTr[buf][:, kt*128:(kt+1)*128], transpose=True),
                "sy", waits=w)
        vt_all_b = kb.qall("sy")
        sqev = kb.emit("scalar", lambda e, buf=buf: e.activation(nsq[:], K2Ta[buf][0:64, :], AF.Square),
                       waits=list(ld_ev) + ([hp_prev_sclear] if hp_prev_sclear else []), inc="se")
        rdev = None
        for rnd in range(4):
            mme = None
            for k2 in range(4):
                cb = rnd * 4 + k2
                w = ()
                if k2 == 0:
                    w = [sqev, vt_last] if rnd == 0 else [rdev]
                mme = kb.emit("tensor", lambda e, cb=cb, k2=k2: e.matmul(
                    psSS[:, k2*512:(k2+1)*512], onesb[0:64, :], nsq[0:64, cb*512:(cb+1)*512],
                    start=True, stop=True), waits=w, inc="te")
            rdev = kb.emit("vector", lambda e, rnd=rnd: e.reduce_max(
                knm[:, rnd:rnd+1], psSS[:], axis=mybir.AxisListType.X), waits=[mme], inc="ve", dr=True)
        k2ev = kb.emit("vector", lambda e: e.reduce_max(knm[:, 4:5], knm[:, 0:4], axis=mybir.AxisListType.X), dr=True, inc="ve")
        ksev = kb.emit("scalar", lambda e: e.activation(knm[:, 5:6], knm[:, 4:5], AF.Sqrt), waits=[k2ev], inc="se")
        kmev = kb.emit("vector", lambda e: e.tensor_scalar(knm[:, 6:7], knm[:, 5:6], -1.0, None, AL.mult),
                       waits=[ksev], inc="ve", dr=True)
        if "cb" in dbg_t and i == 0:
            kb.dma("gpsimd", lambda e: e.dma_start(out=dbg_t["cb"][0:1, 0:8], in_=knm[0:1, 0:8]), "gp", waits=[kmev])
        brow_evs = []
        sq2ev = None
        for h in range(2):
            sq2ev = kb.emit("scalar", lambda e, buf=buf, h=h: e.activation(
                nsq[0:64, 0:T], QTa[buf][0:64, h*T:(h+1)*T], AF.Square),
                waits=list(ld_ev) + ([rdev] if h == 0 else list(brow_evs)), inc="se")
            sqrt_evs = []
            for cb in range(8):
                bank = cb % 4
                w = [sq2ev, kmev] if cb == 0 else ([sqrt_evs[cb - 4]] if cb >= 4 else ())
                mme = kb.emit("tensor", lambda e, cb=cb, bank=bank: e.matmul(
                    psSS[:, bank*512:(bank+1)*512], onesb[0:64, :], nsq[0:64, cb*512:(cb+1)*512],
                    start=True, stop=True), waits=w, inc="te")
                sev_ = kb.emit("scalar", lambda e, cb=cb, bank=bank, h=h, buf=buf: e.activation(
                    QTa[buf][64:65, h*T + cb*512: h*T + (cb+1)*512], psSS[64:65, bank*512:(bank+1)*512], AF.Sqrt),
                    waits=[mme], inc="se")
                sqrt_evs.append(sev_)
            bev_ = kb.emit("vector", lambda e, h=h, buf=buf: e.tensor_scalar(
                QTa[buf][64:65, h*T:(h+1)*T], QTa[buf][64:65, h*T:(h+1)*T],
                knm[64:65, 6:7], 480.0, AL.mult, AL.add), waits=[sqrt_evs[-1]], inc="ve", dr=True)
            brow_evs.append(bev_)
        exp_evs = {}; cons_evs = {}
        mask_evs = {}
        ctx_norm_prev = None
        store_prev = [None, None]
        for q in range(8):
            nkt = 4 * q + 4
            avev = None
            def emit_av(kt, q=q, nkt=nkt, buf=buf):
                eslot = (kt % 4) * 1024
                st = (kt == 0); sp = (kt == nkt - 1)
                mev = mask_evs[kt]
                for h in range(2):
                    kb.emit("tensor", lambda e, h=h, kt=kt, eslot=eslot, st=st, sp=sp, buf=buf: e.matmul(
                        psC[h][0:64, :], Vt[buf][:, kt*128 + h*64: kt*128 + (h+1)*64],
                        Ebuf[:, eslot + h*512: eslot + (h+1)*512], start=st, stop=sp),
                        waits=[mev] if h == 0 else (), inc="te")
                av = kb.emit("tensor", lambda e, kt=kt, eslot=eslot, st=st, sp=sp: (
                    e.matmul(psL2[0][0:64, :], onesb[:, 0:64], Ebuf[:, eslot:eslot+512], start=st, stop=sp),
                    e.matmul(psL2[1][0:64, :], onesb[:, 0:64], Ebuf[:, eslot+512:eslot+1024], start=st, stop=sp))[-1],
                    inc="te")
                cons_evs[kt] = av
                return av
            for kt in range(nkt):
                scol = (kt % 2) * 1024
                eslot = (kt % 4) * 1024
                w = []
                if kt >= 2: w.append(exp_evs[kt - 2])
                if kt == 0:
                    w += [brow_evs[-1], ones_ev] + list(vt_all_b) + list(ld_ev)
                    if ctx_norm_prev is not None: w.append(ctx_norm_prev)
                mm2 = None
                for h in range(2):
                    mm2 = kb.emit("tensor", lambda e, kt=kt, q=q, h=h, scol=scol, buf=buf: e.matmul(
                        psSS[:, scol + h*512: scol + (h+1)*512],
                        K2Ta[buf][0:65, h*T + kt*128: h*T + (kt+1)*128],
                        QTa[buf][0:65, h*T + q*512: h*T + (q+1)*512], start=True, stop=True),
                        waits=w if h == 0 else (), inc="te")
                ew = [mm2]
                if kt >= 4: ew.append(cons_evs[kt - 4])
                eev = kb.emit("scalar", lambda e, kt=kt, scol=scol, eslot=eslot: e.activation(
                    Ebuf[:, eslot:eslot+1024], psSS[:, scol:scol+1024], AF.Exp, scale=0.125),
                    waits=ew, inc="se")
                exp_evs[kt] = eev
                mev = eev
                if kt >= 4 * q:
                    j = kt - 4 * q
                    mev = kb.emit("vector", lambda e, j=j, eslot=eslot: e.tensor_tensor(
                        Ebuf[:, eslot:eslot+1024], Ebuf[:, eslot:eslot+1024], maskB[:, j*1024:(j+1)*1024], AL.mult),
                        waits=[eev], inc="ve", dr=True)
                mask_evs[kt] = mev
                if kt >= 2:
                    avev = emit_av(kt - 2)
            if nkt >= 2:
                avev = emit_av(nkt - 2)
            avev = emit_av(nkt - 1)
            nev = None
            for h in range(2):
                rev = kb.emit("vector", lambda e, h=h: e.tensor_scalar(
                    alb2[:], psL2[h][0:64, :], 1.2e-38, None, AL.max),
                    waits=[avev], inc="ve", dr=True)
                rev = kb.emit("vector", lambda e, h=h: e.reciprocal(alb2[:], alb2[:]), waits=[rev], inc="ve", dr=True)
                nev = kb.emit("vector", lambda e, h=h: e.tensor_tensor(ctxst[h][:], psC[h][0:64, :], alb2[:], AL.mult),
                              waits=[rev] + ([store_prev[h]] if store_prev[h] else []), inc="ve", dr=True)
                dev_ = kb.dma("sync", lambda e, i=i, q=q, h=h: e.dma_start(
                    out=payc[i*128 + h*64: i*128 + (h+1)*64, q*512:(q+1)*512], in_=ctxst[h][:]),
                    "sy", waits=[nev])
                store_prev[h] = dev_
                ctxdma_evs.append(dev_)
            ctx_norm_prev = nev
        hp_prev_sclear = exp_evs[max(exp_evs)]
        hp_done.append(kb.ev("ve"))
        cc2s.append(kb.emit("gpsimd", lambda e, i=i: e.collective_compute(
            "AllGather", AL.bypass, replica_groups=[list(range(8))],
            ins=[payc[i*128:(i+1)*128, :]], outs=[agcs[i]]), waits=kb.qall("sy"), inc="cc"))
    if "ctx" in dbg_t:
        for i in range(4):
            kb.dma("gpsimd", lambda e, i=i: e.dma_start(out=dbg_t["ctx"][i*128:(i+1)*128, :], in_=payc[i*128:(i+1)*128, :]),
                    "gp", waits=kb.qall("sy"))

    cc2ev = cc2s[-1]
    b_done_te = kb.ev("te")
    if upto == "bb":
        kb.emit("sync", lambda e: e.nop(), waits=[cc2ev, kb.ev("se"), kb.ev("ve"), b_done_te] + kb.qall("gp") + kb.qall("sy"))
        kb.finalize()
        return es, dbg_t

    esB.__exit__(None, None, None)
    esC = ExitStack(); esC.__enter__()
    SC = lambda name, shape, dt=F32: esC.enter_context(nc.sbuf_tensor(name, shape, dt))
    PSC = lambda name, shape, dt=F32: esC.enter_context(nc.psum_tensor(name, shape, dt))
    ystg = [SC(f"ystg{k}", [128, 512], BF16) for k in range(2)]
    x2g = SC("x2g", [128, 4 * D])
    xrl = [SC(f"xrl{k}", [128, D]) for k in range(4)]
    hbuf = SC("hbuf", [128, D])
    hb16 = SC("hb16", [128, D], BF16)
    hlo16 = SC("hlo16", [128, D], BF16)
    simf = SC("simf", [128, 8 * 16])
    simhi = SC("simhi", [128, 8 * 16], BF16)
    simlo = SC("simlo", [128, 8 * 16], BF16)
    simsq = SC("simsq", [128, 16])
    gsm = SC("gsm", [128, 24])
    SIGB = SC("SIGB", [128, 16])
    NEG16 = SC("NEG16", [128, 16])
    invncol = SC("invncol", [128, 16])
    lgt = SC("lgt", [16, 512])
    rw = SC("rw", [128, 16 * 16])
    gtile = SC("gtile", [128, 80])
    onesf = SC("onesf", [128, 1])
    qtT = SC("qtT", [128, OWN], BF16)
    qkvb = SC("qkvb", [128, 128], BF16)
    aoT = SC("aoT", [128, OWN], BF16)
    rwb = SC("rwb", [128, 256], BF16)
    rwT = SC("rwT", [16, OWN], BF16)
    aoe = SC("aoe", [128, 16 * 512], BF16)
    kvst = aoe[:, 0:2 * OWN]
    psT2 = PSC("psT2", [128, 2048], BF16)
    psD = PSC("psD", [128, 2048])
    psDa = PSC("psDa", [128, 512])
    psDl = PSC("psDl", [128, 512])
    esOP = ExitStack(); esOP.__enter__()
    SOP = lambda name, shape, dt=F32: esOP.enter_context(nc.sbuf_tensor(name, shape, dt))
    WoTb = SOP("WoTb", [128, 8 * D], BF16)
    ctxo = SOP("ctxo", [128, 8 * OWN], BF16)
    idx2 = SOP("idx2", [128, 16], I32)
    psY = [psD[:, 0:512], psD[:, 512:1024]]

    for dc in range(8):
        kb.dma("gpsimd", lambda e, dc=dc: e.dma_start(out=WoTb[:, dc*D:(dc+1)*D], in_=WoT[dc*128:(dc+1)*128, :]),
               "gp", waits=[cc2ev] if dc == 0 else ())
    wo_ev = kb.qall("gp")
    idx2_ev = kb.dma("sync", lambda e: e.dma_start(out=idx2[:], in_=idx_ctx), "sy", waits=[cc2ev, b_done_te])
    gevs = []
    ccw = [False] * 4
    for dc in range(8):
        for s2 in range(2):
            hpb = dc % 4
            w = []
            if not ccw[hpb]:
                w = [idx2_ev, cc2s[hpb]]; ccw[hpb] = True
            gevs.append(kb.dma("gpsimd", lambda e, dc=dc, s2=s2, hpb=hpb: e.indirect_dma_start(
                out=ctxo[:, dc*OWN + s2*1024: dc*OWN + (s2+1)*1024], out_offset=None,
                in_=agcvs[hpb], in_offset=IndirectOffsetOnAxis(idx2[:, dc*2+s2:dc*2+s2+1], 0)),
                "gp", waits=w))
    ctxo_ev = kb.qall("gp")

    x2_store = []
    spill_evs = {}
    spill_tt = [None, None, None, None]
    for g in range(4):
        xev4 = []
        for tt in range(4):
            t = g * 4 + tt
            w = [spill_evs[g - 1]] if (g >= 1 and tt == 0) else []
            xev4.append(kb.dma("sync", lambda e, t=t, tt=tt: e.dma_start(out=xrl[tt][:], in_=x_own[t*128:(t+1)*128, :]),
                               "sy", waits=w))
        prev_tr = None
        addevs = []
        for oc in range(8):
            ps = psY[oc % 2]
            mmev = None
            for dc in range(8):
                w = ()
                if dc == 0:
                    w = (list(ctxo_ev) + list(wo_ev)) if (g == 0 and oc == 0) else [kb.ev("se")]
                mmev = kb.emit("tensor", lambda e, dc=dc, oc=oc, g=g, ps=ps: e.matmul(
                    ps, WoTb[:, dc*D + oc*128: dc*D + (oc+1)*128],
                    ctxo[:, dc*OWN + g*512:(dc*OWN + (g+1)*512)], start=(dc == 0), stop=(dc == 7)),
                    waits=w, inc="te")
            scev = kb.emit("scalar", lambda e, ps=ps, oc=oc: e.copy(ystg[oc % 2][:], ps), waits=[mmev], inc="se")
            trevs = []
            for tt in range(4):
                w = [scev]
                if prev_tr is not None and tt == 0:
                    w.append(prev_tr)
                trevs.append(kb.emit("tensor", lambda e, tt=tt, oc=oc: e.transpose(
                    psT2[:, (oc % 2)*1024 + tt*256: (oc % 2)*1024 + tt*256 + 128],
                    ystg[oc % 2][:, tt*128:(tt+1)*128], eyeb[:]), waits=w, inc="te"))
            lastadd = None
            for tt in range(4):
                w3 = [trevs[tt], xev4[tt]]
                if oc == 0 and spill_tt[tt] is not None:
                    w3.append(spill_tt[tt])
                lastadd = kb.emit("vector", lambda e, tt=tt, oc=oc, g=g: e.tensor_tensor(
                    x2g[:, tt*D + oc*128: tt*D + (oc+1)*128],
                    psT2[:, (oc % 2)*1024 + tt*256: (oc % 2)*1024 + tt*256 + 128],
                    xrl[tt][:, oc*128:(oc+1)*128], AL.add), waits=[trevs[3]] + w3, inc="ve")
            prev_tr = lastadd
            addevs.append(lastadd)
        sp_ev = None
        for tt in range(4):
            t = g * 4 + tt
            sp_ev = kb.dma("sync", lambda e, t=t, tt=tt: e.dma_start(
                out=x2scr[t*128:(t+1)*128, :], in_=x2g[:, tt*D:(tt+1)*D]), "sy", waits=[addevs[-1]])
            spill_tt[tt] = sp_ev
            if "x2" in dbg_t:
                kb.dma("gpsimd", lambda e, t=t, tt=tt: e.dma_start(
                    out=dbg_t["x2"][t*128:(t+1)*128, :], in_=x2g[:, tt*D:(tt+1)*D]), "gp", waits=[addevs[-1]])
        spill_evs[g] = sp_ev

    if upto == "op":
        kb.emit("sync", lambda e: e.nop(), waits=[kb.ev("se"), kb.ev("ve"), kb.ev("te")] + kb.qall("gp") + kb.qall("sy"))
        kb.finalize()
        return es, dbg_t
    op_done_te = kb.ev("te")
    esOP.__exit__(None, None, None)
    esC2 = ExitStack(); esC2.__enter__()
    SC2_ = lambda name, shape, dt=F32: esC2.enter_context(nc.sbuf_tensor(name, shape, dt))
    psL3 = psDa[0:16, :]
    psG = psDl
    hT = SC2_("hT", [128, 8 * OWN], BF16)
    hloTg = SC2_("hloTg", [128, 8 * 512], BF16)

    kb.emit("gpsimd", lambda e: e.memset(onesf[:], 1.0))
    kb.emit("gpsimd", lambda e: e.memset(NEG16[:], -1e30), inc="gc")
    cmemev = kb.ev("gc")
    simevs = []
    for dc in range(8):
        simevs.append(kb.dma("sync", lambda e, dc=dc: e.dma_start(out=simf[:, dc*16:(dc+1)*16], in_=sim[dc*128:(dc+1)*128, :]), "sy"))
        kb.dma("gpsimd", lambda e, dc=dc: e.dma_start(out=simhi[:, dc*16:(dc+1)*16], in_=sim[dc*128:(dc+1)*128, :]), "gp")
    sim_ev = kb.qall("sy") + kb.qall("gp")
    slo_ev = None
    for dc in range(8):
        slo_ev = kb.emit("vector", lambda e, dc=dc: e.tensor_tensor(
            simlo[:, dc*16:(dc+1)*16], simf[:, dc*16:(dc+1)*16], simhi[:, dc*16:(dc+1)*16], AL.subtract),
            waits=sim_ev if dc == 0 else (), inc="ve", dr=True)
    csev = None
    for dc in range(8):
        sq = kb.emit("scalar", lambda e, dc=dc: e.activation(simsq[:], simf[:, dc*16:(dc+1)*16], AF.Square),
                     waits=[slo_ev] if dc == 0 else [csev], inc="se")
        csev = kb.emit("tensor", lambda e, dc=dc: e.matmul(psDa[0:16, 0:1], simsq[:], onesf[:],
                       start=(dc == 0), stop=(dc == 7)), waits=[sq], inc="te")
    c1 = kb.emit("scalar", lambda e: e.activation(gsm[0:16, 0:1], psDa[0:16, 0:1], AF.Sqrt), waits=[csev], inc="se")
    c2v = kb.emit("vector", lambda e: e.reciprocal(gsm[0:16, 1:2], gsm[0:16, 0:1]), waits=[c1], inc="ve", dr=True)
    gld = kb.dma("sync", lambda e: e.dma_start(out=gsm[0:1, 8:24], in_=gates), "sy")
    sgev = kb.emit("scalar", lambda e: e.activation(gsm[0:1, 8:24].rearrange("a b -> a b"), gsm[0:1, 8:24], AF.Sigmoid),
                   waits=[gld], inc="se")
    sgmm = kb.emit("tensor", lambda e: e.matmul(psDl[:, 0:16], ones1f[0:1, :], gsm[0:1, 8:24], start=True, stop=True),
                   waits=[sgev, cmemev, csev], inc="te")
    sgcp = kb.emit("vector", lambda e: e.tensor_copy(SIGB[:], psDl[:, 0:16]), waits=[sgmm], inc="ve", dr=True)

    rw_evs = []
    hT_evs = []
    prevT = None
    prev_sqv = None
    inv_prev = None
    x2ld_hist = []
    for t in range(NT):
        g, tt = t // 4, t % 4
        pb = (t % 2) * 32
        buf = xa[t % 2]
        w = [spill_evs[g]] if tt == 0 else []
        if t >= 2: w.append(hT_evs[t - 2][1])
        ldev = kb.dma("sync", lambda e, buf=buf, t=t: e.dma_start(out=buf[:], in_=x2scr[t*128:(t+1)*128, :]), "sy", waits=w)
        x2ld_hist.append(ldev)
        ev3, mu3, r3 = ln_stats(buf, pb, [ldev])
        hev = kb.emit("vector", lambda e, buf=buf, mu3=mu3, r3=r3: e.tensor_scalar(hbuf[:], buf[:], mu3, r3, AL.subtract, AL.mult),
                      waits=[ev3] + ([prev_sqv] if prev_sqv else []), inc="ve", dr=True)
        if "h" in dbg_t:
            kb.dma("gpsimd", lambda e, t=t: e.dma_start(out=dbg_t["h"][t*128:(t+1)*128, :], in_=hbuf[:]), "gp", waits=[hev])
        sqv = kb.emit("scalar", lambda e: e.activation(sqscr[:], hbuf[:], AF.Square, accum_out=gsm[:, 2:3]), waits=[hev], inc="se")
        prev_sqv = sqv
        sdv = kb.emit("scalar", lambda e: e.activation(gsm[:, 3:4], gsm[:, 2:3], AF.Sqrt, bias=1e-12),
                      waits=[sqv] + ([inv_prev] if inv_prev else []), inc="se")
        inv = kb.emit("vector", lambda e, t=t: e.reciprocal(invncol[:, t:t+1], gsm[:, 3:4]), waits=[sdv], inc="ve", dr=True)
        inv_prev = inv
        hbe = kb.emit("vector", lambda e: e.tensor_scalar(hb16[:], hbuf[:], 1.0, None, AL.mult),
                      waits=[hev] + ([prevT] if prevT else []), inc="ve", dr=True)
        hle = kb.emit("vector", lambda e: e.tensor_tensor(hlo16[:], hbuf[:], hb16[:], AL.subtract), waits=[hbe], inc="ve", dr=True)
        tev = None
        cp1 = cp2 = None
        for dc in range(8):
            w = [hbe] if dc == 0 else [cp1]
            t1 = kb.emit("tensor", lambda e, dc=dc: e.transpose(psT2[:, 0:128], hb16[:, dc*128:(dc+1)*128], eyeb[:]),
                         waits=w, inc="te")
            cp1 = kb.emit("scalar", lambda e, dc=dc, t=t: e.copy(hT[:, dc*OWN + t*128: dc*OWN + t*128 + 128], psT2[:, 0:128]),
                          waits=[t1], inc="se")
            t2_ = kb.emit("tensor", lambda e, dc=dc: e.transpose(psT2[:, 1024:1152], hlo16[:, dc*128:(dc+1)*128], eyeb[:]),
                          waits=([hle] if dc == 0 else [cp2]), inc="te")
            cp2 = kb.emit("scalar", lambda e, dc=dc, tt=tt: e.copy(hloTg[:, dc*512 + tt*128: dc*512 + tt*128 + 128], psT2[:, 1024:1152]),
                          waits=[t2_], inc="se")
        tev = cp2
        prevT = cp2
        hT_evs.append((tev, kb.ev("se")))
        if tt == 3:
            gmm = None
            idx = 0
            for dc in range(8):
                for pi, (rh,) in enumerate((("hT",), ("hlo",), ("loT",))):
                    st = (idx == 0); sp = (idx == 23); idx += 1
                    w = [tev, sgcp, c2v, slo_ev] if st else ()
                    if st and rw_evs: w.append(rw_evs[-1])
                    gmm = kb.emit("tensor", lambda e, dc=dc, pi=pi, g=g, st=st, sp=sp: e.matmul(
                        psDa[0:16, :],
                        (simhi if pi < 2 else simlo)[:, dc*16:(dc+1)*16],
                        (hloTg[:, dc*512:(dc+1)*512] if pi == 1 else hT[:, dc*OWN + g*512: dc*OWN + (g+1)*512]),
                        start=st, stop=sp), waits=w, inc="te")
            gle = kb.emit("vector", lambda e: e.tensor_scalar(lgt[:], psDa[0:16, :], gsm[0:16, 1:2], None, AL.mult),
                          waits=[gmm], inc="ve", dr=True)
            prev_gt = None
            for tt2 in range(4):
                t2 = g * 4 + tt2
                w = [gle]
                if prev_gt is not None: w.append(prev_gt)
                trev = kb.emit("tensor", lambda e, tt2=tt2: e.transpose(psDl[:, 0:16], lgt[:, tt2*128:(tt2+1)*128], eyef[0:16, 0:16]),
                               waits=w, inc="te")
                gsc = lambda k: gtile[:, 64+k:65+k]
                Lev = kb.emit("vector", lambda e, t2=t2: e.scalar_tensor_tensor(
                    gtile[:, 0:16], psDl[:, 0:16], invncol[:, t2:t2+1], SIGB[:], AL.mult, AL.subtract),
                    waits=[trev], inc="ve", dr=True)
                prev_gt = Lev
                kb.emit("vector", lambda e: e.tensor_scalar(gtile[:, 16:32], gtile[:, 0:16], 0.0, None, AL.max), dr=True)
                kb.emit("vector", lambda e: e.reduce_max(gsc(0), gtile[:, 16:32], axis=mybir.AxisListType.X), dr=True)
                kb.emit("vector", lambda e: e.tensor_scalar(gsc(1), gsc(0), 0.0, None, AL.is_le), dr=True)
                kb.emit("vector", lambda e: e.reduce_max(gsc(2), gtile[:, 0:16], axis=mybir.AxisListType.X), dr=True)
                kb.emit("vector", lambda e: e.tensor_scalar(gtile[:, 32:48], gtile[:, 0:16], gsc(2), None, AL.is_equal), dr=True)
                kb.emit("vector", lambda e: e.scalar_tensor_tensor(gtile[:, 48:64], gtile[:, 32:48], -1e30, gtile[:, 0:16], AL.mult, AL.add), dr=True)
                kb.emit("vector", lambda e: e.reduce_max(gsc(3), gtile[:, 48:64], axis=mybir.AxisListType.X), dr=True)
                kb.emit("vector", lambda e: e.tensor_scalar(gtile[:, 48:64], gtile[:, 48:64], gsc(3), None, AL.is_equal), dr=True)
                kb.emit("vector", lambda e: e.tensor_tensor(gtile[:, 32:48], gtile[:, 32:48], gtile[:, 48:64], AL.add), dr=True)
                kb.emit("vector", lambda e: e.tensor_scalar(gtile[:, 48:64], gtile[:, 16:32], 0.0, None, AL.is_gt), dr=True)
                kb.emit("vector", lambda e: e.scalar_tensor_tensor(gtile[:, 48:64], gtile[:, 32:48], gsc(1), gtile[:, 48:64], AL.mult, AL.add), dr=True)
                kb.emit("vector", lambda e: e.tensor_tensor(gtile[:, 0:16], gtile[:, 16:32], gtile[:, 48:64], AL.mult), dr=True)
                kb.emit("vector", lambda e: e.tensor_scalar(gtile[:, 48:64], gtile[:, 48:64], 1e30, -1e30, AL.mult, AL.add), dr=True)
                kb.emit("vector", lambda e: e.tensor_tensor(gtile[:, 32:48], gtile[:, 0:16], gtile[:, 48:64], AL.add), dr=True)
                kb.emit("vector", lambda e: e.reduce_max(gsc(4), gtile[:, 32:48], axis=mybir.AxisListType.X), dr=True)
                nrmev = kb.emit("vector", lambda e: e.tensor_scalar(gsc(5), gsc(4), -1.0, None, AL.mult), inc="ve", dr=True)
                eev2 = kb.emit("scalar", lambda e: e.activation(gtile[:, 16:32], gtile[:, 32:48], AF.Exp,
                               bias=gsc(5), accum_out=gsc(6)), waits=[nrmev], inc="se")
                rcpe = kb.emit("vector", lambda e: e.reciprocal(gsc(7), gsc(6)), waits=[eev2], inc="ve", dr=True)
                rwe = kb.emit("vector", lambda e, t2=t2: e.tensor_scalar(rw[:, t2*16:(t2+1)*16], gtile[:, 16:32], gsc(7), None, AL.mult),
                              waits=[rcpe], inc="ve", dr=True)
                prev_gt = rwe
            rw_evs.append(rwe)
    c1_done_ve = kb.ev("ve"); c1_done_se = kb.ev("se"); c1_done_te = kb.ev("te")
    if upto == "c1":
        kb.emit("sync", lambda e: e.nop(), waits=[c1_done_se, c1_done_ve, c1_done_te] + kb.qall("gp") + kb.qall("sy"))
        kb.finalize()
        return es, dbg_t

    wstream = [SC2_("wstream", [128, 8 * 2048], BF16)]
    qkvacc = SC2_("qkvacc", [128, 3 * 16 * 128])
    psP = psD
    psP2 = psP
    wevs = {}
    def load_pg(pg, waits):
        wt = (qp, kp, vp)[pg]
        evs = []
        for dc in range(8):
            evs.append(kb.dma("gpsimd", lambda e, pg=pg, dc=dc, wt=wt: e.dma_start(
                out=wstream[0][:, dc*2048:(dc+1)*2048], in_=wt[dc*128:(dc+1)*128, :]),
                "gp", waits=waits if dc == 0 else ()))
        return evs
    wevs[0] = load_pg(0, [op_done_te])
    mix_evs = []
    pg_mm_done = {}
    for pg in range(3):
        if pg >= 1:
            wevs[pg] = load_pg(pg, [pg_mm_done[pg - 1]])
        wv = wevs[pg]
        for t in range(NT):
            for grp in range(8):
                pair = (pg * 128 + t * 8 + grp) % 3
                accs = ((psD[:, 0:128], psD[:, 512:640]),
                        (psD[:, 1024:1152], psD[:, 1536:1664]),
                        (psDa[:, 0:128], psDl[:, 0:128]))[pair]
                mmev = None
                for dc in range(8):
                    for ee in range(2):
                        e_ = grp * 2 + ee
                        w = ()
                        if dc == 0 and ee == 0:
                            w = list(wv) + [rw_evs[t // 4], hT_evs[t][1]]
                            if pair == 2: w.append(rw_evs[3])
                            if len(mix_evs) >= 3: w.append(mix_evs[-3])
                        mmev = kb.emit("tensor", lambda e, dc=dc, ee=ee, e_=e_, t=t, pg=pg, accs=accs: e.matmul(
                            accs[ee],
                            hT[:, dc*OWN + t*128: dc*OWN + (t+1)*128],
                            wstream[0][:, dc*2048 + e_*128: dc*2048 + (e_+1)*128],
                            start=(dc == 0), stop=(dc == 7)), waits=w, inc="te")
                mev2 = None
                for ee in range(2):
                    e_ = grp * 2 + ee
                    if e_ == 0:
                        mev2 = kb.emit("vector", lambda e, ee=ee, t=t, pg=pg, accs=accs: e.tensor_scalar(
                            qkvacc[:, pg*2048 + t*128: pg*2048 + (t+1)*128],
                            accs[ee],
                            rw[:, t*16:t*16+1], None, AL.mult), waits=[mmev], inc="ve", dr=True)
                    else:
                        mev2 = kb.emit("vector", lambda e, ee=ee, e_=e_, t=t, pg=pg, accs=accs: e.scalar_tensor_tensor(
                            qkvacc[:, pg*2048 + t*128: pg*2048 + (t+1)*128],
                            accs[ee],
                            rw[:, t*16+e_:t*16+e_+1], qkvacc[:, pg*2048 + t*128: pg*2048 + (t+1)*128],
                            AL.mult, AL.add), waits=[mmev] if ee == 0 else (), inc="ve", dr=True)
                mix_evs.append(mev2)
        pg_mm_done[pg] = kb.ev("te")
    c2_ve = kb.ev("ve")

    prev3 = None
    c3_evs = []
    for pg in range(3):
        for t in range(NT):
            cev = kb.emit("vector", lambda e, pg=pg, t=t: e.tensor_scalar(
                qkvb[:], qkvacc[:, pg*2048 + t*128: pg*2048 + (t+1)*128], 1.0, None, AL.mult),
                waits=[c2_ve] if (pg == 0 and t == 0) else ([prev3] if prev3 else ()), inc="ve", dr=True)
            trev = kb.emit("tensor", lambda e: e.transpose(psT2[:, 0:128], qkvb[:], eyeb[:]), waits=[cev], inc="te")
            if pg == 0:
                prev3 = kb.emit("scalar", lambda e, t=t: e.copy(qtT[:, t*128:(t+1)*128], psT2[:, 0:128]), waits=[trev], inc="se")
            else:
                prev3 = kb.emit("scalar", lambda e, pg=pg, t=t: e.copy(kvst[:, (pg-1)*OWN + t*128: (pg-1)*OWN + (t+1)*128], psT2[:, 0:128]), waits=[trev], inc="se")
            c3_evs.append(prev3)
    p2evs = []
    for r2 in range(2):
        p2evs.append(kb.dma("sync", lambda e, r2=r2: e.dma_start(
            out=pay2[r2*128:(r2+1)*128, :], in_=kvst[:, r2*OWN:(r2+1)*OWN]), "sy", waits=[c3_evs[-1]]))
    cc3ev = kb.emit("gpsimd", lambda e: e.collective_compute(
        "AllGather", AL.bypass, replica_groups=[list(range(8))], ins=[pay2], outs=[ag2]),
        waits=kb.qall("sy"), inc="cc")

    if upto == "c3":
        kb.emit("sync", lambda e: e.nop(), waits=[cc3ev, kb.ev("se"), kb.ev("ve"), kb.ev("te")] + kb.qall("gp") + kb.qall("sy"))
        kb.finalize()
        return es, dbg_t
    esC2.__exit__(None, None, None)
    SC2 = float(1.0 / np.sqrt(128.0))
    esD = ExitStack(); esD.__enter__()
    SD = lambda name, shape, dt=F32: esD.enter_context(nc.sbuf_tensor(name, shape, dt))
    opb = SD("opb", [128, 16 * D], BF16)
    ktT = SD("ktT", [128, T], BF16)
    nsq2 = SD("nsq2", [128, T], BF16)
    brq16 = SD("brq16", [1, OWN], BF16)
    vtTr = SD("vtTr", [128, T], BF16)
    vtt = SD("vtt", [128, T], BF16)
    E1 = SD("E1", [128, 4 * 512], BF16)
    brq = SD("brq", [1, OWN])
    mld = SD("mld", [128, 128])
    albD = SD("albD", [128, 512])
    mtile = SD("mtile", [128, 512], BF16)
    knm2 = SD("knm2", [128, 8])
    idx3 = SD("idx3", [128, 8], I32)
    idx3_ev = kb.dma("sync", lambda e: e.dma_start(out=idx3[:], in_=idx_kv2), "sy", waits=[cc3ev])
    mld_ev = kb.dma("sync", lambda e: e.dma_start(out=mld[:], in_=mlen), "sy")
    d_g = []
    for t2 in range(2):
        dst = (ktT, vtTr)[t2]
        for g in range(4):
            d_g.append(kb.dma("gpsimd", lambda e, t2=t2, g=g, dst=dst: e.indirect_dma_start(
                out=dst[:, g*1024:(g+1)*1024], out_offset=None,
                in_=ag2v, in_offset=IndirectOffsetOnAxis(idx3[:, t2*4+g:t2*4+g+1], 0)),
                "gp", waits=[idx3_ev, cc3ev] if (t2 == 0 and g == 0) else ()))
    kv2_ev = kb.qall("gp")
    vtev = None
    for kt in range(32):
        vtev = kb.dma("sync", lambda e, kt=kt: e.dma_start(
            out=vtt[:, kt*128:(kt+1)*128], in_=vtTr[:, kt*128:(kt+1)*128], transpose=True),
            "sy", waits=list(kv2_ev) if kt == 0 else ())
    vt_all = kb.qall("sy")
    sqd = kb.emit("scalar", lambda e: e.activation(nsq2[:], ktT[:], AF.Square), waits=list(kv2_ev) + [pg_mm_done[2]], inc="se")
    rdev2 = None
    for rnd in range(2):
        mme = None
        for k2 in range(4):
            cb = rnd * 4 + k2
            w = ([sqd] if rnd == 0 else [rdev2]) if k2 == 0 else ()
            mme = kb.emit("tensor", lambda e, cb=cb, k2=k2: e.matmul(
                psD[:, k2*512:(k2+1)*512], onesb[:], nsq2[:, cb*512:(cb+1)*512],
                start=True, stop=True), waits=w, inc="te")
        rdev2 = kb.emit("vector", lambda e, rnd=rnd: e.reduce_max(
            knm2[:, rnd:rnd+1], psD[:], axis=mybir.AxisListType.X), waits=[mme], inc="ve", dr=True)
    k2ev2 = kb.emit("vector", lambda e: e.reduce_max(knm2[:, 4:5], knm2[:, 0:2], axis=mybir.AxisListType.X), inc="ve", dr=True)
    ksev2 = kb.emit("scalar", lambda e: e.activation(knm2[:, 5:6], knm2[:, 4:5], AF.Sqrt), waits=[k2ev2], inc="se")
    kmev2 = kb.emit("vector", lambda e: e.tensor_scalar(knm2[:, 6:7], knm2[:, 5:6], -1.0, None, AL.mult), waits=[ksev2], inc="ve", dr=True)
    sqd2 = kb.emit("scalar", lambda e: e.activation(nsq2[:, 0:OWN], qtT[:], AF.Square), waits=[mme, c3_evs[15]], inc="se")
    brevs = []
    for cb in range(4):
        bank = cb % 2
        w = [sqd2, kmev2] if cb == 0 else ([brevs[cb - 2]] if cb >= 2 else ())
        mmq = kb.emit("tensor", lambda e, cb=cb, bank=bank: e.matmul(
            psD[:, bank*512:(bank+1)*512], onesb[:], nsq2[:, cb*512:(cb+1)*512],
            start=True, stop=True), waits=w, inc="te")
        sev_ = kb.emit("scalar", lambda e, cb=cb, bank=bank: e.activation(
            brq[0:1, cb*512:(cb+1)*512], psD[0:1, bank*512:(bank+1)*512], AF.Sqrt), waits=[mmq], inc="se")
        brevs.append(sev_)
    brf = kb.emit("vector", lambda e: e.tensor_scalar(brq[0:1, :], brq[0:1, :], knm2[0:1, 6:7], 60.0 / SC2, AL.mult, AL.add),
                  waits=[brevs[-1]], inc="ve", dr=True)
    brb = kb.emit("vector", lambda e: e.tensor_scalar(brq16[0:1, :], brq[0:1, :], 1.0, None, AL.mult), inc="ve", dr=True)
    qbs_slots = D_SLOTS
    exp2 = {}; cons2 = {}
    aost_prev = None
    mev3s = {}
    for sp in range(4):
        nkt = D_SLOTS[sp]
        avev2 = None
        def emit_avd(kt, nkt=nkt):
            esl = (kt % 4) * 512
            st = (kt == 0); spp = (kt == nkt - 1)
            kb.emit("tensor", lambda e, kt=kt, esl=esl, st=st, spp=spp: e.matmul(
                psDa[:], vtt[:, kt*128:(kt+1)*128], E1[:, esl:esl+512], start=st, stop=spp),
                waits=[mev3s[kt]], inc="te")
            av = kb.emit("tensor", lambda e, kt=kt, esl=esl, st=st, spp=spp: e.matmul(
                psDl[:], onesb[:], E1[:, esl:esl+512], start=st, stop=spp), inc="te")
            cons2[kt] = av
            return av
        for kt in range(nkt):
            scol = (kt % 2) * 512
            esl = (kt % 4) * 512
            w = []
            if kt >= 2: w.append(exp2[kt - 2])
            if kt == 0:
                w += [brb] + list(vt_all)
                if aost_prev is not None: w.append(aost_prev)
            mmev = kb.emit("tensor", lambda e, kt=kt, sp=sp, scol=scol: e.matmul(
                psD[:, scol:scol+512], ktT[:, kt*128:(kt+1)*128], qtT[:, sp*512:(sp+1)*512],
                start=True, stop=False), waits=w, inc="te")
            mmev = kb.emit("tensor", lambda e, kt=kt, sp=sp, scol=scol: e.matmul(
                psD[:, scol:scol+512], ones1b[0:1, :], brq16[0:1, sp*512:(sp+1)*512],
                start=False, stop=True), inc="te")
            eev3 = kb.emit("scalar", lambda e, scol=scol, esl=esl: e.activation(
                E1[:, esl:esl+512], psD[:, scol:scol+512], AF.Exp, scale=SC2),
                waits=[mmev] + ([cons2[kt - 4]] if kt >= 4 else []), inc="se")
            exp2[kt] = eev3
            mev3 = kb.emit("vector", lambda e, sp=sp, kt=kt: e.tensor_scalar(
                mtile[:], offsD[:], mld[:, sp*32+kt:sp*32+kt+1], None, AL.is_lt),
                waits=[eev3, mld_ev], inc="ve", dr=True)
            mev3 = kb.emit("vector", lambda e, esl=esl: e.tensor_tensor(
                E1[:, esl:esl+512], E1[:, esl:esl+512], mtile[:], AL.mult),
                waits=[mev3], inc="ve", dr=True)
            mev3s[kt] = mev3
            if kt >= 2:
                avev2 = emit_avd(kt - 2)
        if nkt >= 2:
            avev2 = emit_avd(nkt - 2)
        avev2 = emit_avd(nkt - 1)
        rev2 = kb.emit("vector", lambda e: e.tensor_scalar(albD[:], psDl[:], 1.2e-38, None, AL.max), waits=[avev2], inc="ve", dr=True)
        rev2 = kb.emit("vector", lambda e: e.reciprocal(albD[:], albD[:]), waits=[rev2], inc="ve", dr=True)
        aost_prev = kb.emit("vector", lambda e, sp=sp: e.tensor_tensor(aoT[:, sp*512:(sp+1)*512], psDa[:], albD[:], AL.mult),
                            waits=[rev2], inc="ve", dr=True)
    d_done = kb.ev("ve")
    if "ao" in dbg_t:
        kb.dma("gpsimd", lambda e: e.dma_start(out=dbg_t["ao"][:, 0:OWN], in_=aoT[:]), "gp", waits=[d_done])

    if upto == "d":
        kb.emit("sync", lambda e: e.nop(), waits=[d_done, kb.ev("se"), kb.ev("te")] + kb.qall("gp") + kb.qall("sy"))
        kb.finalize()
        return es, dbg_t
    og = x2g
    for qq in range(4):
        kb.dma("gpsimd", lambda e, qq=qq: e.dma_start(out=opb[:, qq*4096:(qq+1)*4096],
               in_=opw[:, qq*4096:(qq+1)*4096]), "gp", waits=[pg_mm_done[2]] if qq == 0 else ())
    opb_ev = kb.qall("gp")
    rwbe = kb.emit("vector", lambda e: e.tensor_scalar(rwb[:], rw[:], 1.0, None, AL.mult), waits=[c2_ve, d_done], inc="ve", dr=True)
    prev4 = None
    for t in range(NT):
        trev = kb.emit("tensor", lambda e, t=t: e.transpose(psT2[0:16, 0:128], rwb[:, t*16:(t+1)*16], eyeb[:]),
                       waits=[rwbe] if t == 0 else [prev4], inc="te")
        prev4 = kb.emit("scalar", lambda e, t=t: e.copy(rwT[0:16, t*128:(t+1)*128], psT2[0:16, 0:128]), waits=[trev], inc="se")
    rwT_ev = prev4
    out_evs = []
    og_tt = [None, None, None, None]
    og_guard = None
    for g in range(4):
        xev4 = []
        for tt in range(4):
            t = g * 4 + tt
            xev4.append(kb.dma("sync", lambda e, t=t, tt=tt: e.dma_start(out=xrl[tt][:], in_=x2scr[t*128:(t+1)*128, :]), "sy",
                               waits=[d_done] if (g == 0 and tt == 0) else ()))
        aeev = None
        aeh = [None, None]
        for e_ in range(16):
            ps_r = (psDl, psDa)[e_ % 2]
            w = [rwT_ev, d_done] + (list(opb_ev) if g == 0 else []) if e_ == 0 else ([aeh[e_ % 2]] if aeh[e_ % 2] else [])
            rbm = kb.emit("tensor", lambda e, e_=e_, g=g, ps_r=ps_r: e.matmul(
                ps_r[:], oh16[:, e_*128:(e_+1)*128], rwT[0:16, g*512:(g+1)*512], start=True, stop=True),
                waits=w, inc="te")
            aeev = kb.emit("vector", lambda e, e_=e_, g=g, ps_r=ps_r: e.tensor_tensor(
                aoe[:, e_*512:(e_+1)*512], ps_r[:], aoT[:, g*512:(g+1)*512], AL.mult),
                waits=[rbm], inc="ve", dr=True)
            aeh[e_ % 2] = aeev
        prev_tr = None
        addevs = []
        for half in range(2):
            mmv = None
            for e_ in range(16):
                for oc in range(4):
                    occ = half * 4 + oc
                    w = ()
                    if e_ == 0 and oc == 0:
                        w = [aeev, kb.ev("se")]
                    mmv = kb.emit("tensor", lambda e, e_=e_, occ=occ, oc=oc: e.matmul(
                        psD[:, oc*512:(oc+1)*512], opb[:, e_*D + occ*128: e_*D + (occ+1)*128],
                        aoe[:, e_*512:(e_+1)*512], start=(e_ == 0), stop=(e_ == 15)), waits=w, inc="te")
            for oc in range(4):
                occ = half * 4 + oc
                scv = kb.emit("scalar", lambda e, oc=oc: e.copy(ystg[oc % 2][:], psD[:, oc*512:(oc+1)*512]),
                              waits=[mmv] + ([prev_tr] if prev_tr else []), inc="se")
                trl = []
                for tt in range(4):
                    trl.append(kb.emit("tensor", lambda e, tt=tt, oc=oc: e.transpose(
                        psT2[:, (oc % 2)*1024 + tt*256: (oc % 2)*1024 + tt*256 + 128],
                        ystg[oc % 2][:, tt*128:(tt+1)*128], eyeb[:]), waits=[scv], inc="te"))
                lastadd = None
                for tt in range(4):
                    w2 = [trl[tt], xev4[tt]]
                    if half == 0 and oc == 0 and og_tt[tt] is not None:
                        w2.append(og_tt[tt])
                    lastadd = kb.emit("vector", lambda e, tt=tt, occ=occ, oc=oc: e.tensor_tensor(
                        og[:, tt*D + occ*128: tt*D + (occ+1)*128],
                        psT2[:, (oc % 2)*1024 + tt*256: (oc % 2)*1024 + tt*256 + 128],
                        xrl[tt][:, occ*128:(occ+1)*128], AL.add), waits=[trl[3]] + w2, inc="ve")
                prev_tr = lastadd
                addevs.append(lastadd)
        for tt in range(4):
            t = g * 4 + tt
            ev_ = kb.dma("sync", lambda e, t=t, tt=tt: e.dma_start(
                out=out_own[t*128:(t+1)*128, :], in_=og[:, tt*D:(tt+1)*D]), "sy", waits=[addevs[-1]])
            out_evs.append(ev_)
            og_tt[tt] = ev_

    kb.emit("sync", lambda e: e.nop(), waits=[kb.ev("se"), kb.ev("ve"), kb.ev("te")] + kb.qall("gp") + kb.qall("sy"))
    pre = []
    for v, i in cvals.items():
        pre.append(((lambda e, v=v, i=i: e.memset(cpool[:, i:i+1], v)), [], []))
    kb.streams["gpsimd"] = pre + kb.streams["gpsimd"]
    kb.finalize()
    return es, dbg_t


_CACHE = {}

def kernel(**inputs):
    per_core = host_prep(inputs)
    if "nc" not in _CACHE:
        nc = bass.Bass("TRN2", target_bir_lowering=False)
        build(nc)
        _CACHE["nc"] = nc
    nc = _CACHE["nc"]
    in_maps = []
    for c in range(8):
        d = per_core[c]
        m = {k: np.ascontiguousarray(v) for k, v in d.items() if k != "op"}
        m["opw"] = np.ascontiguousarray(d["op"])
        in_maps.append(m)
    res = run_bass_kernel_spmd(nc, in_maps, core_ids=list(range(8)))
    return assemble_out(res.results)
